# version 06
# baseline (speedup 1.0000x reference)
import numpy as np
import ml_dtypes
import concourse.bass as bass
import concourse.mybir as mybir
from concourse.bass import IndirectOffsetOnAxis
from concourse.bass_utils import run_bass_kernel_spmd

F32 = mybir.dt.float32
BF16 = mybir.dt.bfloat16
I32 = mybir.dt.int32
AF = mybir.ActivationFunctionType
ALU = mybir.AluOpType
AX = mybir.AxisListType

HS, SL, HD, NH, NC = 4096, 8192, 128, 32, 8
HPC, SLAB = NH // NC, SL // NC
INTER, RANK = 11008, 1024
NFC = INTER // HD
NHC = HS // HD
RC = RANK // HD
QSC = float(1.0 / np.sqrt(HD))
G_FFN = 7

BF = ml_dtypes.bfloat16




class B:

    DMAQ = ("g", "y")

    def __init__(self, nc):
        self.nc = nc
        self.ops = []
        self.sem = {e: nc.alloc_semaphore(f"ctr_{e}") for e in "tvsgy"}

    def emit(self, eng, fn, reads=(), writes=(), kind="c", tag=""):
        self.ops.append(dict(eng=eng, fn=fn, reads=tuple(reads), writes=tuple(writes),
                             kind=kind, tag=tag, i=len(self.ops), need=False,
                             val=None, deps=[]))

    def dma(self, eng, out_ap, in_ap, reads, writes, tag=""):
        self.emit(eng, lambda e, o=out_ap, i=in_ap: e.dma_start(out=o, in_=i),
                  reads=reads, writes=writes, kind="d", tag=tag)

    def plan(self):
        last_w, readers = {}, {}
        for op in self.ops:
            deps = set()
            for r in op["reads"]:
                w = last_w.get(r)
                if w is not None:
                    deps.add(w)
            for w in op["writes"]:
                lw = last_w.get(w)
                if lw is not None:
                    deps.add(lw)
                deps.update(readers.get(w, ()))
            for r in op["reads"]:
                readers.setdefault(r, set()).add(op["i"])
            for w in op["writes"]:
                last_w[w] = op["i"]
                readers[w] = set()
            deps.discard(op["i"])
            rd = []
            for d in deps:
                p = self.ops[d]
                if p["eng"] == op["eng"] and op["eng"] == "t":
                    continue
                rd.append(d)
            op["deps"] = rd
            for d in rd:
                self.ops[d]["need"] = True
            if op["kind"] in ("d", "cc"):
                op["need"] = True
        cum = {e: 0 for e in "tvsgy"}
        for op in self.ops:
            if op["need"]:
                inc = 1 if (op["eng"] in "tvs" or op["kind"] == "cc") else 16
                cum[op["eng"]] += inc
                op["val"] = cum[op["eng"]]
                op["inc"] = inc
        seen = {c: {p: 0 for p in "tvsgy"} for c in "tvsgy"}
        for op in self.ops:
            req = {}
            for d in op["deps"]:
                p = self.ops[d]
                v = p["val"]
                req[p["eng"]] = max(req.get(p["eng"], 0), v)
            s = seen[op["eng"]]
            op["waits"] = [(e, v) for e, v in req.items() if v > s[e]]
            for e, v in req.items():
                s[e] = max(s[e], v)

    def replay(self, ename, eng):
        for op in self.ops:
            if op["eng"] != ename:
                continue
            for (e, v) in op["waits"]:
                eng.wait_ge(self.sem[e], v)
            if op["fn"] is None:
                continue
            inst = op["fn"](eng)
            if op["need"]:
                inst.then_inc(self.sem[ename], op["inc"])

    def finalize(self):
        self.plan()
        nc = self.nc
        with nc.Block() as block:
            block.tensor(lambda e: self.replay("t", e))
            block.vector(lambda e: self.replay("v", e))
            block.scalar(lambda e: self.replay("s", e))
            block.gpsimd(lambda e: self.replay("g", e))
            block.sync(lambda e: self.replay("y", e))


def build_nc(dbg=False):
    nc = bass.Bass("TRN2", target_bir_lowering=False)

    d_in = {}
    def din(name, shape, dt=F32):
        d_in[name] = nc.dram_tensor(name, shape, dt, kind="ExternalInput").ap()
        return d_in[name]

    xf = din("xf", [HS, SLAB])
    wq = din("wq", [HS, HPC * HD])
    wk = din("wk", [HS, HPC * HD])
    wv = din("wv", [HS, HPC * HD])
    wo_r = din("wo_r", [NHC, 128, HS])
    u1_r = din("u1_r", [RC, 128, HS])
    v1_r = din("v1_r", [2 * NFC, 128, RANK])
    u2_r = din("u2_r", [NFC, 128, RANK])
    v2_r = din("v2_r", [NHC, 128, RANK])
    b1_t = din("b1_t", [128, 2 * NFC])
    b2_t = din("b2_t", [128, NHC])
    cos_t = din("cos_t", [128, SL])
    sin_t = din("sin_t", [128, SL])
    mask_t = din("mask_t", [128, 4 * 512], BF16)
    idx_t = din("idx_t", [128, NHC], I32)

    xn_pre = nc.dram_tensor("xn_pre", [HS, SLAB], BF16, kind="Internal").ap()
    xn_st = nc.dram_tensor("xn_st", [HS, SLAB], BF16, kind="Internal").ap()
    xn_g = nc.dram_tensor("xn_g", [NC * HS, SLAB], BF16, kind="Internal",
                          addr_space="Shared").ap()
    q_st = nc.dram_tensor("q_st", [HPC, 128, SL], BF16, kind="Internal").ap()
    k_st = nc.dram_tensor("k_st", [HPC, 128, SL], BF16, kind="Internal").ap()
    v_st = nc.dram_tensor("v_st", [HPC, 128, SL], BF16, kind="Internal").ap()
    a_pre = nc.dram_tensor("a_pre", [HS, SLAB], BF16, kind="Internal").ap()
    a_st = nc.dram_tensor("a_st", [HS, SLAB], BF16, kind="Internal").ap()
    a_g = nc.dram_tensor("a_g", [NC * HS, SLAB], BF16, kind="Internal",
                         addr_space="Shared").ap()
    out_t = nc.dram_tensor("out", [HS, SLAB], F32, kind="ExternalOutput").ap()
    bar1i = nc.dram_tensor("bar1i", [1, 128], F32, kind="Internal").ap()
    bar1o = nc.dram_tensor("bar1o", [1, 128], F32, kind="Internal").ap()
    bar2i = nc.dram_tensor("bar2i", [1, 128], F32, kind="Internal").ap()
    bar2o = nc.dram_tensor("bar2o", [1, 128], F32, kind="Internal").ap()
    dbg_t = {}
    if dbg:
        for nm, ap_ in (("xn_st", xn_st), ("xn_g", xn_g), ("q_st", q_st),
                        ("k_st", k_st), ("v_st", v_st), ("a_st", a_st),
                        ("a_g", a_g)):
            dbg_t[nm] = nc.dram_tensor("d_" + nm, list(ap_.shape), BF16,
                                       kind="ExternalOutput").ap()

    BASE = ((nc.sbuf_base + 63) // 64) * 64
    TOP = nc.sbuf_top
    alloc_log = []

    def sb(name, shape, dt, off):
        a = nc.alloc_sbuf_tensor_at(name, shape, dt, offset=BASE + off)
        nbytes = int(np.prod(shape[1:])) * mybir.dt.size(dt)
        assert BASE + off + nbytes <= TOP, (name, off, nbytes)
        alloc_log.append((name, off, nbytes, a.name))
        return a

    off = 0
    def bump(n):
        nonlocal off
        r = off
        off = ((off + n + 1023) // 1024) * 1024
        return r
    ones_b = sb("ones_b", [128, 1], BF16, bump(2))
    ones_c = sb("ones_c", [128, 1], F32, bump(4))
    ones_r = sb("ones_r", [1, 128], F32, bump(512))
    masks = sb("masks", [128, 2048], BF16, bump(4096))
    b1s = sb("b1s", [128, 2 * NFC], F32, bump(2 * NFC * 4))
    b2s = sb("b2s", [128, NHC], F32, bump(NHC * 4))
    negc = sb("negc", [128, HPC], F32, bump(HPC * 4))
    nrow = sb("nrow", [1, 1024], F32, bump(4096))
    nrow2 = sb("nrow2", [1, 1024], F32, bump(4096))
    nrow3 = sb("nrow3", [1, 1024], F32, bump(4096))
    nparts = sb("nparts", [1, 256], F32, bump(1024))
    epsb = sb("epsb", [1, 1], F32, bump(4))
    idxs = sb("idxs", [128, NHC], I32, bump(NHC * 4))
    PERSIST = ((off + 1023) // 1024) * 1024

    def region(*specs):
        o = PERSIST
        out = []
        for (name, shape, dt) in specs:
            nbytes = int(np.prod(shape[1:])) * mybir.dt.size(dt)
            out.append(sb(name, shape, dt, o))
            o = ((o + nbytes + 63) // 64) * 64
        assert BASE + o <= TOP, ("region overflow", o)
        return out

    xch0, xch1, sqch, bc_r, bc_b, xnb0, xnb1 = region(
        ("xch0", [128, 1024], F32), ("xch1", [128, 1024], F32),
        ("sqch", [128, 1024], F32),
        ("bc_r", [128, 1024], F32), ("bc_b", [128, 1024], F32),
        ("xnb0", [128, 1024], BF16), ("xnb1", [128, 1024], BF16))
    xch = [xch0, xch1]
    xnb = [xnb0, xnb1]

    (wq_s, wk_s, cosb0, cosb1, sinb0, sinb1, xng0, xng1, qf32, rotb, t1b, sqb,
     outb0, outb1, outb2, outb3) = region(
        ("wq_s", [128, NHC * 512], BF16), ("wk_s", [128, NHC * 512], BF16),
        ("cosb0", [128, 512], F32), ("cosb1", [128, 512], F32),
        ("sinb0", [128, 512], F32), ("sinb1", [128, 512], F32),
        ("xng0", [128, 8 * 512], BF16), ("xng1", [128, 8 * 512], BF16),
        ("qf32", [128, 512], F32), ("rotb", [128, 512], F32),
        ("t1b", [128, 512], F32), ("sqb", [128, 512], BF16),
        ("outb0", [128, 512], BF16), ("outb1", [128, 512], BF16),
        ("outb2", [128, 512], BF16), ("outb3", [128, 512], BF16))
    cosb, sinb = [cosb0, cosb1], [sinb0, sinb1]
    xng = [xng0, xng1]
    outb = [outb0, outb1, outb2, outb3]
    wv_s = wk_s

    (qh0, qh1, kh0, kh1, vh0, vh1, aoh0, aoh1, ptb0, ptb1, bcs0, bcs1) = region(
        ("qh0", [128, SL], BF16), ("qh1", [128, SL], BF16),
        ("kh0", [128, SL], BF16), ("kh1", [128, SL], BF16),
        ("vh0", [128, SL], BF16), ("vh1", [128, SL], BF16),
        ("aoh0", [128, SL], BF16), ("aoh1", [128, SL], BF16),
        ("ptb0", [128, 1024], BF16), ("ptb1", [128, 1024], BF16),
        ("bcs0", [128, 512], F32), ("bcs1", [128, 512], F32))
    qh, kh, vh, aoh = [qh0, qh1], [kh0, kh1], [vh0, vh1], [aoh0, aoh1]
    ptb, bcs = [ptb0, ptb1], [bcs0, bcs1]

    ffn_specs = [("xno", [128, NHC * 1024], BF16),
                 ("u1b0", [128, HS], BF16),
                 ("p1", [128, RC * 1024], BF16),
                 ("zS", [128, RC * 1024], F32),
                 ("zB", [128, RC * 1024], BF16),
                 ("v1g0", [128, 1024], BF16), ("v1g1", [128, 1024], BF16),
                 ("v1u0", [128, 1024], BF16), ("v1u1", [128, 1024], BF16),
                 ("silub", [128, 1024], F32)]
    ffn_specs += [(f"actg{i}", [128, 1024], BF16) for i in range(G_FFN)]
    ffn_specs += [(f"u2g{i}", [128, 1024], BF16) for i in range(G_FFN)]
    ffn_t = region(*ffn_specs)
    (xno, u1b0, p1, zS, zB, v1g0, v1g1, v1u0, v1u1, silub) = ffn_t[:10]
    actg = ffn_t[10:10 + G_FFN]
    u2g = ffn_t[10 + G_FFN:]
    u1b = [u1b0, u1b0]
    v1g, v1u = [v1g0, v1g1], [v1u0, v1u1]

    (aslab, wob0, wob1, v2b0, v2b1, xfc0, xfc1, outS0, outS1) = region(
        ("aslab", [128, NHC * 1024], BF16),
        ("wob0", [128, HS], BF16), ("wob1", [128, HS], BF16),
        ("v2b0", [128, 1024], BF16), ("v2b1", [128, 1024], BF16),
        ("xfc0", [128, 1024], F32), ("xfc1", [128, 1024], F32),
        ("outS0", [128, 1024], F32), ("outS1", [128, 1024], F32))
    wob, v2b = [wob0, wob1], [v2b0, v2b1]
    xfc, outS = [xfc0, xfc1], [outS0, outS1]

    offs = {n: o for (n, o, _, _) in alloc_log}
    zb_off = offs["zB"]
    fin_end = offs["outS1"] + 4096
    assert fin_end <= zb_off, (fin_end, zb_off)

    pages = {}
    for (name, o, nb, real) in alloc_log:
        pg = tuple(("P", p) for p in range(o // 1024, (o + nb + 1023) // 1024))
        pages[name] = pg
        pages[real] = pg

    def R(*ts):
        out = []
        for t in ts:
            out.extend(pages[t if isinstance(t, str) else t.name])
        return out

    pp = [nc.alloc_psum_tensor(f"pp{i}", [128, 1024], F32) for i in range(4)]
    pb = [pp[i // 2].ap()[:, (i % 2) * 512:(i % 2 + 1) * 512] for i in range(8)]
    PS = [("ps", i) for i in range(8)]

    b = B(nc)
    E, D = b.emit, b.dma

    E("v", lambda e: e.memset(ones_b[:, :], 1.0), writes=R("ones_b"))
    E("v", lambda e: e.memset(ones_c[:, :], 1.0), writes=R("ones_c"))
    E("v", lambda e: e.memset(ones_r[:, :], 1.0), writes=R("ones_r"))
    E("v", lambda e: e.memset(epsb[:, :], 1e-5), writes=R("epsb"))
    D("y", masks[:, :], mask_t, reads=[("dr", "mask_t")], writes=R("masks"))
    D("y", b1s[:, :], b1_t, reads=[("dr", "b1_t")], writes=R("b1s"))
    D("y", b2s[:, :], b2_t, reads=[("dr", "b2_t")], writes=R("b2s"))
    D("y", idxs[:, :], idx_t, reads=[("dr", "idx_t")], writes=R("idxs"))

    for ch in range(NHC):
        c = xch[ch % 2]
        D("y", c[:, :], xf[ch * 128:(ch + 1) * 128, :],
          reads=[("dr", "xf")], writes=R(c.name))
        E("s", lambda e, c=c: e.activation(sqch[:, :], c[:, :], AF.Square),
          reads=R(c.name), writes=R("sqch"))
        st, sp = (ch == 0), (ch == NHC - 1)
        for half in range(2):
            E("t", lambda e, c=c, h=half, st=st, sp=sp: e.matmul(
                pb[h][0:1, :], ones_c[:, :], c[:, h * 512:(h + 1) * 512],
                start=st, stop=sp),
              reads=R(c.name, "ones_c"), writes=[PS[half]])
            E("t", lambda e, h=half, st=st, sp=sp: e.matmul(
                pb[2 + h][0:1, :], ones_c[:, :], sqch[:, h * 512:(h + 1) * 512],
                start=st, stop=sp),
              reads=R("sqch", "ones_c"), writes=[PS[2 + half]])
    for half in range(2):
        E("v", lambda e, h=half: e.tensor_scalar_mul(
            nrow[0:1, h * 512:(h + 1) * 512], pb[h][0:1, :], 1.0 / HS),
          reads=[PS[half]], writes=R("nrow"))
        E("v", lambda e, h=half: e.tensor_scalar_mul(
            nrow2[0:1, h * 512:(h + 1) * 512], pb[2 + h][0:1, :], 1.0 / HS),
          reads=[PS[2 + half]], writes=R("nrow2"))
    E("v", lambda e: e.tensor_mul(nrow3[0:1, :], nrow[0:1, :], nrow[0:1, :]),
      reads=R("nrow"), writes=R("nrow3"))
    E("v", lambda e: e.tensor_sub(nrow2[0:1, :], nrow2[0:1, :], nrow3[0:1, :]),
      reads=R("nrow2", "nrow3"), writes=R("nrow2"))
    E("s", lambda e: e.activation(nrow3[0:1, :], nrow2[0:1, :], AF.Sqrt,
                                  bias=epsb[0:1, 0:1]),
      reads=R("nrow2", "epsb"), writes=R("nrow3"))
    E("v", lambda e: e.reciprocal(nrow2[0:1, :], nrow3[0:1, :]),
      reads=R("nrow3"), writes=R("nrow2"))
    E("v", lambda e: e.tensor_mul(nrow3[0:1, :], nrow[0:1, :], nrow2[0:1, :]),
      reads=R("nrow", "nrow2"), writes=R("nrow3"))
    E("v", lambda e: e.tensor_scalar_mul(nrow3[0:1, :], nrow3[0:1, :], -1.0),
      reads=R("nrow3"), writes=R("nrow3"))
    for half in range(2):
        E("t", lambda e, h=half: e.matmul(
            pb[4 + h][:, :], ones_r[:, :], nrow2[0:1, h * 512:(h + 1) * 512],
            start=True, stop=True),
          reads=R("nrow2", "ones_r"), writes=[PS[4 + half]])
        E("t", lambda e, h=half: e.matmul(
            pb[6 + h][:, :], ones_r[:, :], nrow3[0:1, h * 512:(h + 1) * 512],
            start=True, stop=True),
          reads=R("nrow3", "ones_r"), writes=[PS[6 + half]])
        E("s", lambda e, h=half: e.copy(bc_r[:, h * 512:(h + 1) * 512], pb[4 + h][:, :]),
          reads=[PS[4 + half]], writes=R("bc_r"))
        E("s", lambda e, h=half: e.copy(bc_b[:, h * 512:(h + 1) * 512], pb[6 + h][:, :]),
          reads=[PS[6 + half]], writes=R("bc_b"))
    for ch in range(NHC):
        c = xch[ch % 2]
        o = xnb[ch % 2]
        D("y", c[:, :], xf[ch * 128:(ch + 1) * 128, :],
          reads=[("dr", "xf")], writes=R(c.name))
        E("v", lambda e, c=c: e.tensor_mul(sqch[:, :], c[:, :], bc_r[:, :]),
          reads=R(c.name, "bc_r"), writes=R("sqch"))
        E("v", lambda e, o=o: e.tensor_add(o[:, :], sqch[:, :], bc_b[:, :]),
          reads=R("sqch", "bc_b"), writes=R(o.name))
        D("g", xn_pre[ch * 128:(ch + 1) * 128, :], o[:, :],
          reads=R(o.name), writes=[("dr", "xn_pre")])

    E("g", lambda e: e.dma_start(out=xn_st, in_=xn_pre),
      reads=[("dr", "xn_pre")], writes=[("dr", "xn_st")], kind="c")
    E("g", lambda e: e.collective_compute(
        "AllGather", ALU.bypass, replica_groups=[list(range(NC))],
        ins=[xn_st], outs=[xn_g]),
      reads=[("dr", "xn_st")], writes=[("dr", "xn_g")], kind="cc")
    E("g", lambda e: e.collective_compute(
        "AllReduce", ALU.add, replica_groups=[list(range(NC))],
        ins=[bar1i], outs=[bar1o]),
      reads=[("dr", "xn_g")], writes=[("dr", "xn_g"), ("dr", "bar1")], kind="cc")

    D("g", wq_s[:, :], wq.rearrange("(hc p) d -> p hc d", p=128),
      reads=[("dr", "wq")], writes=R("wq_s"))
    D("g", wk_s[:, :], wk.rearrange("(hc p) d -> p hc d", p=128),
      reads=[("dr", "wk")], writes=R("wk_s"))

    def qk_evac(mat, hh, tt, bank):
        wsl = ("q" if mat == 0 else "k")
        st = q_st if mat == 0 else k_st
        scale = QSC if mat == 0 else 1.0
        ob = outb[(hh + 4 * (mat)) % 4]
        cb, sbn = cosb[tt % 2], sinb[tt % 2]
        E("s", lambda e: e.activation(qf32[:, :], pb[bank][:, :], AF.Copy, scale=scale),
          reads=[PS[bank]], writes=R("qf32"))
        E("v", lambda e: e.tensor_scalar_mul(rotb[0:64, :], qf32[64:128, :], -1.0),
          reads=R("qf32"), writes=R("rotb"))
        E("v", lambda e: e.tensor_copy(rotb[64:128, :], qf32[0:64, :]),
          reads=R("qf32"), writes=R("rotb"))
        E("v", lambda e, cb=cb: e.tensor_mul(t1b[:, :], qf32[:, :], cb[:, :]),
          reads=R("qf32", cb.name), writes=R("t1b"))
        E("v", lambda e, sbn=sbn: e.tensor_mul(rotb[:, :], rotb[:, :], sbn[:, :]),
          reads=R("rotb", sbn.name), writes=R("rotb"))
        E("v", lambda e, ob=ob: e.tensor_add(ob[:, :], t1b[:, :], rotb[:, :]),
          reads=R("t1b", "rotb"), writes=R(ob.name))
        D("g", st[hh, :, tt * 512:(tt + 1) * 512], ob[:, :],
          reads=R(ob.name), writes=[("dr", f"{wsl}st{hh}")])
        E("v", lambda e, ob=ob: e.tensor_mul(sqb[:, :], ob[:, :], ob[:, :]),
          reads=R(ob.name), writes=R("sqb"))
        E("t", lambda e: e.matmul(pb[bank][0:1, :], ones_b[:, :], sqb[:, :],
                                  start=True, stop=True),
          reads=R("sqb", "ones_b"), writes=[PS[bank]])
        slot = (mat * HPC + hh) * 16 + tt
        E("v", lambda e, slot=slot: e.reduce_max(
            nparts[0:1, slot:slot + 1], pb[bank][0:1, :], axis=AX.X),
          reads=[PS[bank]], writes=R("nparts"))

    for tt in range(16):
        r, half = tt // 2, tt % 2
        D("y", cosb[tt % 2][:, :], cos_t[:, tt * 512:(tt + 1) * 512],
          reads=[("dr", "cos_t")], writes=R(cosb[tt % 2].name))
        D("y", sinb[tt % 2][:, :], sin_t[:, tt * 512:(tt + 1) * 512],
          reads=[("dr", "sin_t")], writes=R(sinb[tt % 2].name))
        for grp in range(4):
            xg = xng[grp % 2]
            src = xn_g[r * HS + grp * 1024:r * HS + (grp + 1) * 1024,
                       half * 512:(half + 1) * 512]
            D("y", xg[:, :], src.rearrange("(hc p) t -> p hc t", p=128),
              reads=[("dr", "xn_g")], writes=R(xg.name))
            for hcl in range(8):
                hc = grp * 8 + hcl
                rhs = xg[:, hcl * 512:(hcl + 1) * 512]
                for mat, ws in ((0, wq_s), (1, wk_s)):
                    for hh in range(HPC):
                        bank = mat * 4 + hh
                        E("t", lambda e, ws=ws, hc=hc, hh=hh, rhs=rhs, bank=bank,
                                 st=(hc == 0), sp=(hc == NHC - 1): e.matmul(
                            pb[bank][:, :], ws[:, hc * 512 + hh * 128:hc * 512 + (hh + 1) * 128],
                            rhs, start=st, stop=sp),
                          reads=R(ws.name, xg.name), writes=[PS[bank]])
        for mat in range(2):
            for hh in range(HPC):
                qk_evac(mat, hh, tt, mat * 4 + hh)

    D("g", wv_s[:, :], wv.rearrange("(hc p) d -> p hc d", p=128),
      reads=[("dr", "wv")], writes=R(wv_s.name))
    for tt in range(16):
        r, half = tt // 2, tt % 2
        for grp in range(4):
            xg = xng[grp % 2]
            src = xn_g[r * HS + grp * 1024:r * HS + (grp + 1) * 1024,
                       half * 512:(half + 1) * 512]
            D("y", xg[:, :], src.rearrange("(hc p) t -> p hc t", p=128),
              reads=[("dr", "xn_g")], writes=R(xg.name))
            for hcl in range(8):
                hc = grp * 8 + hcl
                for ts in range(4):
                    E("t", lambda e, hc=hc, hcl=hcl, ts=ts, xg=xg,
                             st=(hc == 0), sp=(hc == NHC - 1): e.matmul(
                        pb[ts][:, :],
                        xg[:, hcl * 512 + ts * 128:hcl * 512 + (ts + 1) * 128],
                        wv_s[:, hc * 512:(hc + 1) * 512], start=st, stop=sp),
                      reads=R(xg.name, wv_s.name), writes=[PS[ts]])
        for ts in range(4):
            gts = tt * 4 + ts
            ob = outb[ts]
            E("s", lambda e, ts=ts, ob=ob: e.activation(ob[:, :], pb[ts][:, :], AF.Copy),
              reads=[PS[ts]], writes=R(ob.name))
            D("g", v_st[0:HPC, :, gts * 128:(gts + 1) * 128].transpose([1, 0, 2]),
              ob.ap().rearrange("p (h d) -> p h d", h=HPC),
              reads=R(ob.name), writes=[("dr", "vst")])

    for hh in range(HPC):
        E("v", lambda e, hh=hh: e.reduce_max(
            nrow[0:1, hh:hh + 1], nparts[0:1, hh * 16:(hh + 1) * 16], axis=AX.X),
          reads=R("nparts"), writes=R("nrow"))
        E("v", lambda e, hh=hh: e.reduce_max(
            nrow[0:1, 8 + hh:9 + hh],
            nparts[0:1, (HPC + hh) * 16:(HPC + hh + 1) * 16], axis=AX.X),
          reads=R("nparts"), writes=R("nrow"))
    E("v", lambda e: e.tensor_mul(nrow2[0:1, 0:HPC], nrow[0:1, 0:HPC], nrow[0:1, 8:8 + HPC]),
      reads=R("nrow"), writes=R("nrow2"))
    E("s", lambda e: e.activation(nrow3[0:1, 0:HPC], nrow2[0:1, 0:HPC], AF.Sqrt),
      reads=R("nrow2"), writes=R("nrow3"))
    E("v", lambda e: e.tensor_scalar_mul(nrow3[0:1, 0:HPC], nrow3[0:1, 0:HPC], -1.0),
      reads=R("nrow3"), writes=R("nrow3"))
    E("t", lambda e: e.matmul(pb[7][:, 0:HPC], ones_r[:, :], nrow3[0:1, 0:HPC],
                              start=True, stop=True),
      reads=R("nrow3", "ones_r"), writes=[PS[7]])
    E("s", lambda e: e.copy(negc[:, :], pb[7][:, 0:HPC]),
      reads=[PS[7]], writes=R("negc"))

    for hh in range(HPC):
        qb_, kb_, vb_, ab_ = qh[hh % 2], kh[hh % 2], vh[hh % 2], aoh[hh % 2]
        D("y", qb_[:, :], q_st[hh, :, :], reads=[("dr", f"qst{hh}")], writes=R(qb_.name))
        D("y", kb_[:, :], k_st[hh, :, :], reads=[("dr", f"kst{hh}")], writes=R(kb_.name))
        D("y", vb_[:, :], v_st[hh, :, :], reads=[("dr", "vst")], writes=R(vb_.name))
        for qb in range(16):
            qs = slice(qb * 512, (qb + 1) * 512)
            npair = 2 * (qb + 1)

            def emit_S(p, hh=hh, kb_=kb_, qb_=qb_, qs=qs, npair=npair):
                sP = pp[p % 2]
                sB = [PS[2 * (p % 2)], PS[2 * (p % 2) + 1]]
                pt = ptb[p % 2]
                for j in range(2):
                    k0 = (2 * p + j) * 128
                    E("t", lambda e, sP=sP, j=j, kb_=kb_, qb_=qb_, k0=k0, qs=qs:
                        e.matmul(sP[:, j * 512:(j + 1) * 512],
                                 kb_[:, k0:k0 + 128], qb_[:, qs],
                                 start=True, stop=True),
                      reads=R(kb_.name, qb_.name), writes=[sB[j]])
                E("s", lambda e, pt=pt, sP=sP, hh=hh: e.activation(
                    pt[:, :], sP[:, :], AF.Exp, bias=negc[:, hh:hh + 1]),
                  reads=sB + R("negc"), writes=R(pt.name))
                if p >= npair - 2:
                    moff = (p - (npair - 2)) * 1024
                    E("v", lambda e, pt=pt, moff=moff: e.tensor_mul(
                        pt[:, :], pt[:, :], masks[:, moff:moff + 1024]),
                      reads=R(pt.name, "masks"), writes=R(pt.name))

            acc_b = 4 if qb % 2 == 0 else 6
            den_b = 5 if qb % 2 == 0 else 7

            def emit_PV(p, vb_=vb_, npair=npair, acc_b=acc_b, den_b=den_b):
                pt = ptb[p % 2]
                first, last = (p == 0), (p == npair - 1)
                for j in range(2):
                    k0 = (2 * p + j) * 128
                    E("t", lambda e, vb_=vb_, pt=pt, k0=k0, j=j, acc_b=acc_b,
                             st=(first and j == 0), sp=(last and j == 1): e.matmul(
                        pb[acc_b][:, :], vb_[:, k0:k0 + 128],
                        pt[:, j * 512:(j + 1) * 512], start=st, stop=sp),
                      reads=R(vb_.name, pt.name), writes=[PS[acc_b]])
                    E("t", lambda e, pt=pt, j=j, den_b=den_b,
                             st=(first and j == 0), sp=(last and j == 1): e.matmul(
                        pb[den_b][0:1, :], ones_b[:, :],
                        pt[:, j * 512:(j + 1) * 512], start=st, stop=sp),
                      reads=R(pt.name, "ones_b"), writes=[PS[den_b]])

            emit_S(0)
            for p in range(npair):
                if p + 1 < npair:
                    emit_S(p + 1)
                emit_PV(p)
            rrow = nrow if qb % 2 == 0 else nrow2
            E("v", lambda e, den_b=den_b, rrow=rrow: e.reciprocal(
                rrow[0:1, 0:512], pb[den_b][0:1, :]),
              reads=[PS[den_b]], writes=R(rrow.name))
            E("t", lambda e, den_b=den_b, rrow=rrow: e.matmul(
                pb[den_b][:, :], ones_r[:, :], rrow[0:1, 0:512],
                start=True, stop=True),
              reads=R(rrow.name, "ones_r"), writes=[PS[den_b]])
            bc = bcs[qb % 2]
            E("s", lambda e, bc=bc, den_b=den_b: e.copy(bc[:, :], pb[den_b][:, :]),
              reads=[PS[den_b]], writes=R(bc.name))
            E("v", lambda e, ab_=ab_, qs=qs, bc=bc, acc_b=acc_b: e.tensor_mul(
                ab_[:, qs], pb[acc_b][:, :], bc[:, :]),
              reads=[PS[acc_b]] + R(bc.name), writes=R(ab_.name))
        D("g", a_pre.rearrange("(j f) t -> f j t", j=NC)[hh * 128:(hh + 1) * 128, :, :],
          ab_.ap().rearrange("p (j t) -> p j t", j=NC),
          reads=R(ab_.name), writes=[("dr", "a_pre")])

    E("g", lambda e: e.dma_start(out=a_st, in_=a_pre),
      reads=[("dr", "a_pre")], writes=[("dr", "a_st")], kind="c")
    E("g", lambda e: e.collective_compute(
        "AllGather", ALU.bypass, replica_groups=[list(range(NC))],
        ins=[a_st], outs=[a_g]),
      reads=[("dr", "a_st")], writes=[("dr", "a_g")], kind="cc")
    E("g", lambda e: e.collective_compute(
        "AllReduce", ALU.add, replica_groups=[list(range(NC))],
        ins=[bar2i], outs=[bar2o]),
      reads=[("dr", "a_g")], writes=[("dr", "a_g"), ("dr", "bar2")], kind="cc")

    E("v", lambda e: e.memset(zS[:, :], 0.0), writes=R("zS"))
    for half2 in range(2):
        D("y", xno.ap()[:, half2 * 16 * 1024:(half2 + 1) * 16 * 1024],
          xn_st[half2 * 2048:(half2 + 1) * 2048, :].rearrange("(hc p) t -> p hc t", p=128),
          reads=[("dr", "xn_st")], writes=R("xno"))
    for rc in range(RC):
        ub = u1b[rc % 2]
        D("g", ub[:, :], u1_r[rc], reads=[("dr", "u1_r")], writes=R(ub.name))
        for half in range(2):
            bank = 2 * (rc % 2) + half
            for hc in range(NHC):
                E("t", lambda e, ub=ub, hc=hc, half=half, bank=bank,
                         st=(hc == 0), sp=(hc == NHC - 1): e.matmul(
                    pb[bank][:, :], ub[:, hc * 128:(hc + 1) * 128],
                    xno[:, hc * 1024 + half * 512:hc * 1024 + (half + 1) * 512],
                    start=st, stop=sp),
                  reads=R(ub.name, "xno"), writes=[PS[bank]])
            E("s", lambda e, rc=rc, half=half, bank=bank: e.copy(
                p1[:, rc * 1024 + half * 512:rc * 1024 + (half + 1) * 512],
                pb[bank][:, :]),
              reads=[PS[bank]], writes=R("p1"))
    fgroups = []
    i0 = 0
    while i0 < NFC:
        fgroups.append(list(range(i0, min(i0 + G_FFN, NFC))))
        i0 += G_FFN
    for grp in fgroups:
        for gi, i in enumerate(grp):
            vg, vu = v1g[i % 2], v1u[i % 2]
            ag = actg[gi]
            D("g", vg[:, :], v1_r[i], reads=[("dr", "v1_r")], writes=R(vg.name))
            D("g", vu[:, :], v1_r[NFC + i], reads=[("dr", "v1_r")], writes=R(vu.name))
            D("g", u2g[gi][:, :], u2_r[i], reads=[("dr", "u2_r")],
              writes=R(u2g[gi].name))
            for half in range(2):
                gb, ub_ = half, 2 + half
                for rc in range(RC):
                    E("t", lambda e, vg=vg, rc=rc, half=half, gb=gb,
                             st=(rc == 0), sp=(rc == RC - 1): e.matmul(
                        pb[gb][:, :], vg[:, rc * 128:(rc + 1) * 128],
                        p1[:, rc * 1024 + half * 512:rc * 1024 + (half + 1) * 512],
                        start=st, stop=sp),
                      reads=R(vg.name, "p1"), writes=[PS[gb]])
                    E("t", lambda e, vu=vu, rc=rc, half=half, ub_=ub_,
                             st=(rc == 0), sp=(rc == RC - 1): e.matmul(
                        pb[ub_][:, :], vu[:, rc * 128:(rc + 1) * 128],
                        p1[:, rc * 1024 + half * 512:rc * 1024 + (half + 1) * 512],
                        start=st, stop=sp),
                      reads=R(vu.name, "p1"), writes=[PS[ub_]])
                E("s", lambda e, half=half, gb=gb, i=i: e.activation(
                    silub[:, half * 512:(half + 1) * 512], pb[gb][:, :], AF.Silu,
                    bias=b1s[:, i:i + 1]),
                  reads=[PS[gb]] + R("b1s"), writes=R("silub"))
                E("v", lambda e, ag=ag, half=half, ub_=ub_, i=i: e.scalar_tensor_tensor(
                    ag[:, half * 512:(half + 1) * 512], pb[ub_][:, :],
                    b1s[:, NFC + i:NFC + i + 1],
                    silub[:, half * 512:(half + 1) * 512], ALU.add, ALU.mult),
                  reads=[PS[ub_]] + R("b1s", "silub"), writes=R(ag.name))
        for rc in range(RC):
            for half in range(2):
                bank = 4 + half
                for gi, i in enumerate(grp):
                    ag = actg[gi]
                    ub2 = u2g[gi]
                    E("t", lambda e, ub2=ub2, ag=ag, rc=rc, half=half, bank=bank,
                             st=(gi == 0), sp=(gi == len(grp) - 1): e.matmul(
                        pb[bank][:, :], ub2[:, rc * 128:(rc + 1) * 128],
                        ag[:, half * 512:(half + 1) * 512], start=st, stop=sp),
                      reads=R(ub2.name, ag.name), writes=[PS[bank]])
                E("v", lambda e, rc=rc, half=half, bank=bank: e.tensor_add(
                    zS[:, rc * 1024 + half * 512:rc * 1024 + (half + 1) * 512],
                    zS[:, rc * 1024 + half * 512:rc * 1024 + (half + 1) * 512],
                    pb[bank][:, :]),
                  reads=[PS[bank]] + R("zS"), writes=R("zS"))
    E("v", lambda e: e.tensor_copy(zB[:, :], zS[:, :]), reads=R("zS"), writes=R("zB"))

    for hi in range(NHC):
        E("g", lambda e, hi=hi: e.indirect_dma_start(
            aslab[:, hi * 1024:(hi + 1) * 1024], None, a_g,
            IndirectOffsetOnAxis(idxs[:, hi:hi + 1], 0)),
          reads=[("dr", "a_g")] + R("idxs"), writes=R("aslab"), kind="d")
    for ho in range(NHC):
        wb = wob[ho % 2]
        v2 = v2b[ho % 2]
        xc = xfc[ho % 2]
        oS = outS[ho % 2]
        D("g", wb[:, :], wo_r[ho], reads=[("dr", "wo_r")], writes=R(wb.name))
        D("g", v2[:, :], v2_r[ho], reads=[("dr", "v2_r")], writes=R(v2.name))
        D("y", xc[:, :], xf[ho * 128:(ho + 1) * 128, :],
          reads=[("dr", "xf")], writes=R(xc.name))
        for half in range(2):
            bank = 2 * (ho % 2) + half
            for hi in range(NHC):
                E("t", lambda e, wb=wb, hi=hi, half=half, bank=bank,
                         st=(hi == 0): e.matmul(
                    pb[bank][:, :], wb[:, hi * 128:(hi + 1) * 128],
                    aslab[:, hi * 1024 + half * 512:hi * 1024 + (half + 1) * 512],
                    start=st, stop=False),
                  reads=R(wb.name, "aslab"), writes=[PS[bank]])
            for rc in range(RC):
                E("t", lambda e, v2=v2, rc=rc, half=half, bank=bank,
                         sp=(rc == RC - 1): e.matmul(
                    pb[bank][:, :], v2[:, rc * 128:(rc + 1) * 128],
                    zB[:, rc * 1024 + half * 512:rc * 1024 + (half + 1) * 512],
                    start=False, stop=sp),
                  reads=R(v2.name, "zB"), writes=[PS[bank]])
            E("v", lambda e, oS=oS, xc=xc, half=half, bank=bank, ho=ho: e.scalar_tensor_tensor(
                oS[:, half * 512:(half + 1) * 512], pb[bank][:, :],
                b2s[:, ho:ho + 1], xc[:, half * 512:(half + 1) * 512],
                ALU.add, ALU.add),
              reads=[PS[bank]] + R(xc.name, "b2s"), writes=R(oS.name))
        D("y", out_t[ho * 128:(ho + 1) * 128, :], oS[:, :],
          reads=R(oS.name), writes=[("dr", "out")])
    if dbg:
        for nm, src_ap in (("xn_st", xn_st), ("xn_g", xn_g), ("q_st", q_st),
                           ("k_st", k_st), ("v_st", v_st), ("a_st", a_st),
                           ("a_g", a_g)):
            D("g", dbg_t[nm], src_ap, reads=[("dr", nm.replace("q_st", "qst").replace("k_st", "kst").replace("v_st", "vst"))],
              writes=[("dr", "d" + nm)])
    dbg_reads = [("dr", "d" + nm) for nm in dbg_t] if dbg else []
    E("y", None, reads=[("dr", "out")] + dbg_reads)

    b.finalize()
    return nc




_cache = {}


def _host_prep():
    if "c" in _cache:
        return _cache["c"]
    inv_freq = 1.0 / 10000.0 ** (np.arange(0, HD, 2, dtype=np.float32) / HD)
    pos = np.arange(SL, dtype=np.float32)
    emb = np.concatenate([pos[:, None] * inv_freq[None, :]] * 2, -1)
    cos_t = np.ascontiguousarray(np.cos(emb).astype(np.float32).T)
    sin_t = np.ascontiguousarray(np.sin(emb).astype(np.float32).T)
    m = np.zeros((128, 4 * 512), np.float32)
    for kt in range(4):
        m[:, kt * 512:(kt + 1) * 512] = (
            kt * 128 + np.arange(128)[:, None] <= np.arange(512)[None, :])
    mask_t = m.astype(BF)
    idxs = []
    for c in range(NC):
        idx = np.zeros((128, NHC), np.int32)
        for hi in range(NHC):
            r, hh = hi // HPC, hi % HPC
            idx[:, hi] = r * HS + c * 512 + hh * 128 + np.arange(128)
        idxs.append(idx)
    _cache["c"] = (cos_t, sin_t, mask_t, idxs)
    return _cache["c"]


def kernel(L0_attn_hidden_states, L0_attn_wq, L0_attn_wk, L0_attn_wv, L0_attn_wo,
           L0_ffn_U1, L0_ffn_V1, L0_ffn_U2, L0_ffn_V2, L0_ffn_b1, L0_ffn_b2):
    x = np.asarray(L0_attn_hidden_states, np.float32).reshape(SL, HS)
    wq = np.asarray(L0_attn_wq, np.float32)
    wk = np.asarray(L0_attn_wk, np.float32)
    wv = np.asarray(L0_attn_wv, np.float32)
    wo = np.asarray(L0_attn_wo, np.float32)
    U1 = np.asarray(L0_ffn_U1, np.float32)
    V1 = np.asarray(L0_ffn_V1, np.float32)
    U2 = np.asarray(L0_ffn_U2, np.float32)
    V2 = np.asarray(L0_ffn_V2, np.float32)
    b1 = np.asarray(L0_ffn_b1, np.float32)
    b2 = np.asarray(L0_ffn_b2, np.float32)

    cos_t, sin_t, mask_t, idxs = _host_prep()
    wo_r = np.ascontiguousarray(
        wo.reshape(NHC, 128, NHC, 128).transpose(2, 1, 0, 3).reshape(NHC, 128, HS))
    u1_r = np.ascontiguousarray(
        U1.reshape(NHC, 128, RC, 128).transpose(2, 1, 0, 3).reshape(RC, 128, HS))
    v1_r = np.ascontiguousarray(
        V1.reshape(RC, 128, 2 * NFC, 128).transpose(2, 1, 0, 3).reshape(2 * NFC, 128, RANK))
    u2_r = np.ascontiguousarray(U2.reshape(NFC, 128, RANK))
    v2_r = np.ascontiguousarray(
        V2.reshape(RC, 128, NHC, 128).transpose(2, 1, 0, 3).reshape(NHC, 128, RANK))
    b1_t = np.ascontiguousarray(b1.reshape(2 * NFC, 128).T)
    b2_t = np.ascontiguousarray(b2.reshape(NHC, 128).T)

    nc = build_nc()
    in_maps = []
    for c in range(NC):
        hsl = slice(c * HPC * HD, (c + 1) * HPC * HD)
        in_maps.append(dict(
            xf=np.ascontiguousarray(x[c * SLAB:(c + 1) * SLAB].T),
            wq=np.ascontiguousarray(wq[:, hsl]),
            wk=np.ascontiguousarray(wk[:, hsl]),
            wv=np.ascontiguousarray(wv[:, hsl]),
            wo_r=wo_r, u1_r=u1_r, v1_r=v1_r, u2_r=u2_r, v2_r=v2_r,
            b1_t=b1_t, b2_t=b2_t, cos_t=cos_t, sin_t=sin_t,
            mask_t=mask_t, idx_t=idxs[c]))
    res = run_bass_kernel_spmd(nc, in_maps, core_ids=list(range(NC)))
    out = np.concatenate([res.results[c]["out"].T for c in range(NC)], 0)
    return out.reshape(1, SL, HS)


# version 09
# speedup vs baseline: 1.1403x; 1.1403x over previous
import numpy as np
import ml_dtypes
import concourse.bass as bass
import concourse.mybir as mybir
from concourse.bass import IndirectOffsetOnAxis
from concourse.bass_utils import run_bass_kernel_spmd

F32 = mybir.dt.float32
BF16 = mybir.dt.bfloat16
I32 = mybir.dt.int32
AF = mybir.ActivationFunctionType
ALU = mybir.AluOpType
AX = mybir.AxisListType

HS, SL, HD, NH, NC = 4096, 8192, 128, 32, 8
HPC, SLAB = NH // NC, SL // NC
INTER, RANK = 11008, 1024
NFC = INTER // HD
NHC = HS // HD
RC = RANK // HD
QSC = float(1.0 / np.sqrt(HD))
G_FFN = 7

BF = ml_dtypes.bfloat16




class B:

    DMAQ = ("g", "y")

    def __init__(self, nc):
        self.nc = nc
        self.ops = []
        self.sem = {e: nc.alloc_semaphore(f"ctr_{e}") for e in "tvsgy"}

    def emit(self, eng, fn, reads=(), writes=(), kind="c", tag=""):
        self.ops.append(dict(eng=eng, fn=fn, reads=tuple(reads), writes=tuple(writes),
                             kind=kind, tag=tag, i=len(self.ops), need=False,
                             val=None, deps=[]))

    def dma(self, eng, out_ap, in_ap, reads, writes, tag=""):
        self.emit(eng, lambda e, o=out_ap, i=in_ap: e.dma_start(out=o, in_=i),
                  reads=reads, writes=writes, kind="d", tag=tag)

    def plan(self):
        last_w, readers = {}, {}
        for op in self.ops:
            deps = set()
            for r in op["reads"]:
                w = last_w.get(r)
                if w is not None:
                    deps.add(w)
            for w in op["writes"]:
                lw = last_w.get(w)
                if lw is not None:
                    deps.add(lw)
                deps.update(readers.get(w, ()))
            for r in op["reads"]:
                readers.setdefault(r, set()).add(op["i"])
            for w in op["writes"]:
                last_w[w] = op["i"]
                readers[w] = set()
            deps.discard(op["i"])
            rd = []
            for d in deps:
                p = self.ops[d]
                if p["eng"] == op["eng"] and op["eng"] == "t":
                    continue
                rd.append(d)
            op["deps"] = rd
            for d in rd:
                self.ops[d]["need"] = True
            if op["kind"] in ("d", "cc"):
                op["need"] = True
        cum = {e: 0 for e in "tvsgy"}
        for op in self.ops:
            if op["need"]:
                inc = 1 if (op["eng"] in "tvs" or op["kind"] == "cc") else 16
                cum[op["eng"]] += inc
                op["val"] = cum[op["eng"]]
                op["inc"] = inc
        seen = {c: {p: 0 for p in "tvsgy"} for c in "tvsgy"}
        for op in self.ops:
            req = {}
            for d in op["deps"]:
                p = self.ops[d]
                v = p["val"]
                req[p["eng"]] = max(req.get(p["eng"], 0), v)
            s = seen[op["eng"]]
            op["waits"] = [(e, v) for e, v in req.items() if v > s[e]]
            for e, v in req.items():
                s[e] = max(s[e], v)

    def replay(self, ename, eng):
        for op in self.ops:
            if op["eng"] != ename:
                continue
            for (e, v) in op["waits"]:
                eng.wait_ge(self.sem[e], v)
            if op["fn"] is None:
                continue
            inst = op["fn"](eng)
            if op["need"]:
                inst.then_inc(self.sem[ename], op["inc"])

    def finalize(self):
        self.plan()
        nc = self.nc
        with nc.Block() as block:
            block.tensor(lambda e: self.replay("t", e))
            block.vector(lambda e: self.replay("v", e))
            block.scalar(lambda e: self.replay("s", e))
            block.gpsimd(lambda e: self.replay("g", e))
            block.sync(lambda e: self.replay("y", e))


def build_nc(dbg=False):
    nc = bass.Bass("TRN2", target_bir_lowering=False)

    d_in = {}
    def din(name, shape, dt=F32):
        d_in[name] = nc.dram_tensor(name, shape, dt, kind="ExternalInput").ap()
        return d_in[name]

    xf = din("xf", [HS, SLAB])
    wq = din("wq", [HS, HPC * HD])
    wk = din("wk", [HS, HPC * HD])
    wv = din("wv", [HS, HPC * HD])
    wo_r = din("wo_r", [NHC, 128, HS])
    u1_r = din("u1_r", [RC, 128, HS])
    v1_r = din("v1_r", [2 * NFC, 128, RANK])
    u2_r = din("u2_r", [NFC, 128, RANK])
    v2_r = din("v2_r", [NHC, 128, RANK])
    b1_t = din("b1_t", [128, 2 * NFC])
    b2_t = din("b2_t", [128, NHC])
    cos_t = din("cos_t", [128, SL])
    sin_t = din("sin_t", [128, SL])
    mask_t = din("mask_t", [128, 4 * 512], BF16)
    idx_t = din("idx_t", [128, NHC], I32)

    xn_pre = nc.dram_tensor("xn_pre", [HS, SLAB], BF16, kind="Internal").ap()
    xn_st = nc.dram_tensor("xn_st", [HS, SLAB], BF16, kind="Internal").ap()
    xn_g = nc.dram_tensor("xn_g", [NC * HS, SLAB], BF16, kind="Internal",
                          addr_space="Shared").ap()
    q_st = nc.dram_tensor("q_st", [HPC, 128, SL], BF16, kind="Internal").ap()
    k_st = nc.dram_tensor("k_st", [HPC, 128, SL], BF16, kind="Internal").ap()
    v_st = nc.dram_tensor("v_st", [HPC, 128, SL], BF16, kind="Internal").ap()
    a_pre = nc.dram_tensor("a_pre", [HS, SLAB], BF16, kind="Internal").ap()
    a_st = nc.dram_tensor("a_st", [HS, SLAB], BF16, kind="Internal").ap()
    a_g = nc.dram_tensor("a_g", [NC * HS, SLAB], BF16, kind="Internal",
                         addr_space="Shared").ap()
    out_t = nc.dram_tensor("out", [HS, SLAB], F32, kind="ExternalOutput").ap()
    bar1i = nc.dram_tensor("bar1i", [1, 128], F32, kind="Internal").ap()
    bar1o = nc.dram_tensor("bar1o", [1, 128], F32, kind="Internal").ap()
    bar2i = nc.dram_tensor("bar2i", [1, 128], F32, kind="Internal").ap()
    aslab_d = nc.dram_tensor("aslab_d", [128, NHC * 1024], BF16, kind="Internal").ap()
    bar2o = nc.dram_tensor("bar2o", [1, 128], F32, kind="Internal").ap()
    dbg_t = {}
    if dbg:
        for nm, ap_ in (("xn_st", xn_st), ("xn_g", xn_g), ("q_st", q_st),
                        ("k_st", k_st), ("v_st", v_st), ("a_st", a_st),
                        ("a_g", a_g)):
            dbg_t[nm] = nc.dram_tensor("d_" + nm, list(ap_.shape), BF16,
                                       kind="ExternalOutput").ap()

    BASE = ((nc.sbuf_base + 63) // 64) * 64
    TOP = nc.sbuf_top
    alloc_log = []

    def sb(name, shape, dt, off):
        a = nc.alloc_sbuf_tensor_at(name, shape, dt, offset=BASE + off)
        nbytes = int(np.prod(shape[1:])) * mybir.dt.size(dt)
        assert BASE + off + nbytes <= TOP, (name, off, nbytes)
        alloc_log.append((name, off, nbytes, a.name))
        return a

    off = 0
    def bump(n):
        nonlocal off
        r = off
        off = ((off + n + 1023) // 1024) * 1024
        return r
    ones_b = sb("ones_b", [128, 1], BF16, bump(2))
    ones_c = sb("ones_c", [128, 1], F32, bump(4))
    ones_r = sb("ones_r", [1, 128], F32, bump(512))
    masks = sb("masks", [128, 2048], BF16, bump(4096))
    b1s = sb("b1s", [128, 2 * NFC], F32, bump(2 * NFC * 4))
    b2s = sb("b2s", [128, NHC], F32, bump(NHC * 4))
    negc = sb("negc", [128, HPC], F32, bump(HPC * 4))
    nrow = sb("nrow", [1, 1024], F32, bump(4096))
    nrow2 = sb("nrow2", [1, 1024], F32, bump(4096))
    nrow3 = sb("nrow3", [1, 1024], F32, bump(4096))
    nparts = sb("nparts", [1, 256], F32, bump(1024))
    epsb = sb("epsb", [1, 1], F32, bump(4))
    idxs = sb("idxs", [128, NHC], I32, bump(NHC * 4))
    PERSIST = ((off + 1023) // 1024) * 1024

    def region(*specs):
        specs = list(specs)
        o = PERSIST
        out = []
        if specs and isinstance(specs[0], int):
            o = PERSIST + specs[0]
            specs = specs[1:]
        for (name, shape, dt) in specs:
            nbytes = int(np.prod(shape[1:])) * mybir.dt.size(dt)
            out.append(sb(name, shape, dt, o))
            o = ((o + nbytes + 63) // 64) * 64
        assert BASE + o <= TOP, ("region overflow", o)
        return out

    xch0, xch1, sqch, bc_r, bc_b, xnb0, xnb1 = region(
        ("xch0", [128, 1024], F32), ("xch1", [128, 1024], F32),
        ("sqch", [128, 1024], F32),
        ("bc_r", [128, 1024], F32), ("bc_b", [128, 1024], F32),
        ("xnb0", [128, 1024], BF16), ("xnb1", [128, 1024], BF16))
    xch = [xch0, xch1]
    xnb = [xnb0, xnb1]

    (wq_s, wk_s, cosb0, cosb1, sinb0, sinb1, xng0, xng1, qf32, rotb, t1b, sqb,
     outb0, outb1, outb2, outb3) = region(
        ("wq_s", [128, NHC * 512], BF16), ("wk_s", [128, NHC * 512], BF16),
        ("cosb0", [128, 512], F32), ("cosb1", [128, 512], F32),
        ("sinb0", [128, 512], F32), ("sinb1", [128, 512], F32),
        ("xng0", [128, 8 * 512], BF16), ("xng1", [128, 8 * 512], BF16),
        ("qf32", [128, 512], F32), ("rotb", [128, 512], F32),
        ("t1b", [128, 512], F32), ("sqb", [128, 512], BF16),
        ("outb0", [128, 512], BF16), ("outb1", [128, 512], BF16),
        ("outb2", [128, 512], BF16), ("outb3", [128, 512], BF16))
    cosb, sinb = [cosb0, cosb1], [sinb0, sinb1]
    xng = [xng0, xng1]
    outb = [outb0, outb1, outb2, outb3]
    wv_s = wk_s

    (qh0, qh1, kh0, kh1, vh0, vh1, aoh0, aoh1, ptb0, ptb1, bcs0, bcs1) = region(
        ("qh0", [128, SL], BF16), ("qh1", [128, SL], BF16),
        ("kh0", [128, SL], BF16), ("kh1", [128, SL], BF16),
        ("vh0", [128, SL], BF16), ("vh1", [128, SL], BF16),
        ("aoh0", [128, SL], BF16), ("aoh1", [128, SL], BF16),
        ("ptb0", [128, 1024], BF16), ("ptb1", [128, 1024], BF16),
        ("bcs0", [128, 512], F32), ("bcs1", [128, 512], F32))
    qh, kh, vh, aoh = [qh0, qh1], [kh0, kh1], [vh0, vh1], [aoh0, aoh1]
    ptb, bcs = [ptb0, ptb1], [bcs0, bcs1]
    (sqq,) = region(137 * 1024, ("sqq", [128, 2048], BF16))

    ffn_specs = [("xno", [128, NHC * 1024], BF16),
                 ("p1", [128, RC * 1024], BF16),
                 ("zS", [128, RC * 1024], F32)]
    ffn_specs += [(f"actg{i}", [128, 1024], BF16) for i in range(G_FFN)]
    ffn_t = region(*ffn_specs)
    (xno, p1, zS) = ffn_t[:3]
    actg = ffn_t[3:]
    ffn_s = [141 * 1024,
             ("u1b0", [128, HS], BF16),
             ("v1g0", [128, 1024], BF16), ("v1g1", [128, 1024], BF16),
             ("v1u0", [128, 1024], BF16), ("v1u1", [128, 1024], BF16),
             ("silub", [128, 1024], F32)]
    ffn_s += [(f"u2g{i}", [128, 1024], BF16) for i in range(G_FFN)]
    ffn_st = region(*ffn_s)
    (u1b0, v1g0, v1g1, v1u0, v1u1, silub) = ffn_st[:6]
    u2g = ffn_st[6:6 + G_FFN]
    (zB,) = region(126 * 1024, ("zB", [128, RC * 1024], BF16))
    u1b = [u1b0, u1b0]
    v1g, v1u = [v1g0, v1g1], [v1u0, v1u1]

    (aslab, wob0, wob1, v2b0, v2b1, xfc0, xfc1, outS0, outS1) = region(
        ("aslab", [128, NHC * 1024], BF16),
        ("wob0", [128, HS], BF16), ("wob1", [128, HS], BF16),
        ("v2b0", [128, 1024], BF16), ("v2b1", [128, 1024], BF16),
        ("xfc0", [128, 1024], F32), ("xfc1", [128, 1024], F32),
        ("outS0", [128, 1024], F32), ("outS1", [128, 1024], F32))
    wob, v2b = [wob0, wob1], [v2b0, v2b1]
    xfc, outS = [xfc0, xfc1], [outS0, outS1]

    offs = {n: o for (n, o, _, _) in alloc_log}
    zb_off = offs["zB"]
    fin_end = offs["outS1"] + 4096
    assert fin_end <= zb_off, (fin_end, zb_off)

    pages = {}
    for (name, o, nb, real) in alloc_log:
        pg = tuple(("P", p) for p in range(o // 1024, (o + nb + 1023) // 1024))
        pages[name] = pg
        pages[real] = pg

    def R(*ts):
        out = []
        for t in ts:
            out.extend(pages[t if isinstance(t, str) else t.name])
        return out

    pp = [nc.alloc_psum_tensor(f"pp{i}", [128, 1024], F32) for i in range(4)]
    pb = [pp[i // 2].ap()[:, (i % 2) * 512:(i % 2 + 1) * 512] for i in range(8)]
    PS = [("ps", i) for i in range(8)]

    b = B(nc)
    E, D = b.emit, b.dma

    E("v", lambda e: e.memset(ones_b[:, :], 1.0), writes=R("ones_b"))
    E("v", lambda e: e.memset(ones_c[:, :], 1.0), writes=R("ones_c"))
    E("v", lambda e: e.memset(ones_r[:, :], 1.0), writes=R("ones_r"))
    E("v", lambda e: e.memset(epsb[:, :], 1e-5), writes=R("epsb"))
    D("y", masks[:, :], mask_t, reads=[("dr", "mask_t")], writes=R("masks"))
    D("y", b1s[:, :], b1_t, reads=[("dr", "b1_t")], writes=R("b1s"))
    D("y", b2s[:, :], b2_t, reads=[("dr", "b2_t")], writes=R("b2s"))
    D("y", idxs[:, :], idx_t, reads=[("dr", "idx_t")], writes=R("idxs"))

    for ch in range(NHC):
        c = xch[ch % 2]
        D("y", c[:, :], xf[ch * 128:(ch + 1) * 128, :],
          reads=[("dr", "xf")], writes=R(c.name))
        E("s", lambda e, c=c: e.activation(sqch[:, :], c[:, :], AF.Square),
          reads=R(c.name), writes=R("sqch"))
        st, sp = (ch == 0), (ch == NHC - 1)
        for half in range(2):
            E("t", lambda e, c=c, h=half, st=st, sp=sp: e.matmul(
                pb[h][0:1, :], ones_c[:, :], c[:, h * 512:(h + 1) * 512],
                start=st, stop=sp),
              reads=R(c.name, "ones_c"), writes=[PS[half]])
            E("t", lambda e, h=half, st=st, sp=sp: e.matmul(
                pb[2 + h][0:1, :], ones_c[:, :], sqch[:, h * 512:(h + 1) * 512],
                start=st, stop=sp),
              reads=R("sqch", "ones_c"), writes=[PS[2 + half]])
    for half in range(2):
        E("v", lambda e, h=half: e.tensor_scalar_mul(
            nrow[0:1, h * 512:(h + 1) * 512], pb[h][0:1, :], 1.0 / HS),
          reads=[PS[half]], writes=R("nrow"))
        E("v", lambda e, h=half: e.tensor_scalar_mul(
            nrow2[0:1, h * 512:(h + 1) * 512], pb[2 + h][0:1, :], 1.0 / HS),
          reads=[PS[2 + half]], writes=R("nrow2"))
    E("v", lambda e: e.tensor_mul(nrow3[0:1, :], nrow[0:1, :], nrow[0:1, :]),
      reads=R("nrow"), writes=R("nrow3"))
    E("v", lambda e: e.tensor_sub(nrow2[0:1, :], nrow2[0:1, :], nrow3[0:1, :]),
      reads=R("nrow2", "nrow3"), writes=R("nrow2"))
    E("s", lambda e: e.activation(nrow3[0:1, :], nrow2[0:1, :], AF.Sqrt,
                                  bias=epsb[0:1, 0:1]),
      reads=R("nrow2", "epsb"), writes=R("nrow3"))
    E("v", lambda e: e.reciprocal(nrow2[0:1, :], nrow3[0:1, :]),
      reads=R("nrow3"), writes=R("nrow2"))
    E("v", lambda e: e.tensor_mul(nrow3[0:1, :], nrow[0:1, :], nrow2[0:1, :]),
      reads=R("nrow", "nrow2"), writes=R("nrow3"))
    E("v", lambda e: e.tensor_scalar_mul(nrow3[0:1, :], nrow3[0:1, :], -1.0),
      reads=R("nrow3"), writes=R("nrow3"))
    for half in range(2):
        E("t", lambda e, h=half: e.matmul(
            pb[4 + h][:, :], ones_r[:, :], nrow2[0:1, h * 512:(h + 1) * 512],
            start=True, stop=True),
          reads=R("nrow2", "ones_r"), writes=[PS[4 + half]])
        E("t", lambda e, h=half: e.matmul(
            pb[6 + h][:, :], ones_r[:, :], nrow3[0:1, h * 512:(h + 1) * 512],
            start=True, stop=True),
          reads=R("nrow3", "ones_r"), writes=[PS[6 + half]])
        E("s", lambda e, h=half: e.copy(bc_r[:, h * 512:(h + 1) * 512], pb[4 + h][:, :]),
          reads=[PS[4 + half]], writes=R("bc_r"))
        E("s", lambda e, h=half: e.copy(bc_b[:, h * 512:(h + 1) * 512], pb[6 + h][:, :]),
          reads=[PS[6 + half]], writes=R("bc_b"))
    for ch in range(NHC):
        c = xch[ch % 2]
        o = xnb[ch % 2]
        D("y", c[:, :], xf[ch * 128:(ch + 1) * 128, :],
          reads=[("dr", "xf")], writes=R(c.name))
        E("v", lambda e, c=c: e.tensor_mul(sqch[:, :], c[:, :], bc_r[:, :]),
          reads=R(c.name, "bc_r"), writes=R("sqch"))
        E("v", lambda e, o=o: e.tensor_add(o[:, :], sqch[:, :], bc_b[:, :]),
          reads=R("sqch", "bc_b"), writes=R(o.name))
        D("g", xn_pre[ch * 128:(ch + 1) * 128, :], o[:, :],
          reads=R(o.name), writes=[("dr", "xn_pre")])

    E("g", lambda e: e.dma_start(out=xn_st, in_=xn_pre),
      reads=[("dr", "xn_pre")], writes=[("dr", "xn_st")], kind="c")
    D("g", wq_s[:, :], wq.rearrange("(hc p) d -> p hc d", p=128),
      reads=[("dr", "wq")], writes=R("wq_s"))
    D("g", wk_s[:, :], wk.rearrange("(hc p) d -> p hc d", p=128),
      reads=[("dr", "wk")], writes=R("wk_s"))
    E("g", lambda e: e.collective_compute(
        "AllGather", ALU.bypass, replica_groups=[list(range(NC))],
        ins=[xn_st], outs=[xn_g]),
      reads=[("dr", "xn_st")], writes=[("dr", "xn_g")], kind="cc")
    E("g", lambda e: e.collective_compute(
        "AllReduce", ALU.add, replica_groups=[list(range(NC))],
        ins=[bar1i], outs=[bar1o]),
      reads=[("dr", "xn_g")], writes=[("dr", "xn_g"), ("dr", "bar1")], kind="cc")


    def qk_evac(mat, hh, tt, bank):
        wsl = ("q" if mat == 0 else "k")
        st = q_st if mat == 0 else k_st
        scale = QSC if mat == 0 else 1.0
        ob = outb[(hh + 4 * (mat)) % 4]
        cb, sbn = cosb[tt % 2], sinb[tt % 2]
        E("s", lambda e: e.activation(qf32[:, :], pb[bank][:, :], AF.Copy, scale=scale),
          reads=[PS[bank]], writes=R("qf32"))
        E("v", lambda e: e.tensor_scalar_mul(rotb[0:64, :], qf32[64:128, :], -1.0),
          reads=R("qf32"), writes=R("rotb"))
        E("v", lambda e: e.tensor_copy(rotb[64:128, :], qf32[0:64, :]),
          reads=R("qf32"), writes=R("rotb"))
        E("v", lambda e, cb=cb: e.tensor_mul(t1b[:, :], qf32[:, :], cb[:, :]),
          reads=R("qf32", cb.name), writes=R("t1b"))
        E("v", lambda e, sbn=sbn: e.tensor_mul(rotb[:, :], rotb[:, :], sbn[:, :]),
          reads=R("rotb", sbn.name), writes=R("rotb"))
        E("v", lambda e, ob=ob: e.tensor_add(ob[:, :], t1b[:, :], rotb[:, :]),
          reads=R("t1b", "rotb"), writes=R(ob.name))
        D("g", st[hh, :, tt * 512:(tt + 1) * 512], ob[:, :],
          reads=R(ob.name), writes=[("dr", f"{wsl}st{hh}")])

    for tt in range(16):
        r, half = tt // 2, tt % 2
        D("y", cosb[tt % 2][:, :], cos_t[:, tt * 512:(tt + 1) * 512],
          reads=[("dr", "cos_t")], writes=R(cosb[tt % 2].name))
        D("y", sinb[tt % 2][:, :], sin_t[:, tt * 512:(tt + 1) * 512],
          reads=[("dr", "sin_t")], writes=R(sinb[tt % 2].name))
        for grp in range(4):
            xg = xng[grp % 2]
            src = xn_g[r * HS + grp * 1024:r * HS + (grp + 1) * 1024,
                       half * 512:(half + 1) * 512]
            D("y", xg[:, :], src.rearrange("(hc p) t -> p hc t", p=128),
              reads=[("dr", "xn_g")], writes=R(xg.name))
            for hcl in range(8):
                hc = grp * 8 + hcl
                rhs = xg[:, hcl * 512:(hcl + 1) * 512]
                for mat, ws in ((0, wq_s), (1, wk_s)):
                    for hh in range(HPC):
                        bank = mat * 4 + hh
                        E("t", lambda e, ws=ws, hc=hc, hh=hh, rhs=rhs, bank=bank,
                                 st=(hc == 0), sp=(hc == NHC - 1): e.matmul(
                            pb[bank][:, :], ws[:, hc * 512 + hh * 128:hc * 512 + (hh + 1) * 128],
                            rhs, start=st, stop=sp),
                          reads=R(ws.name, xg.name), writes=[PS[bank]])
        for mat in range(2):
            for hh in range(HPC):
                qk_evac(mat, hh, tt, mat * 4 + hh)

    D("g", wv_s[:, :], wv.rearrange("(hc p) d -> p hc d", p=128),
      reads=[("dr", "wv")], writes=R(wv_s.name))
    for tt in range(16):
        r, half = tt // 2, tt % 2
        for grp in range(4):
            xg = xng[grp % 2]
            src = xn_g[r * HS + grp * 1024:r * HS + (grp + 1) * 1024,
                       half * 512:(half + 1) * 512]
            D("y", xg[:, :], src.rearrange("(hc p) t -> p hc t", p=128),
              reads=[("dr", "xn_g")], writes=R(xg.name))
            for hcl in range(8):
                hc = grp * 8 + hcl
                for ts in range(4):
                    E("t", lambda e, hc=hc, hcl=hcl, ts=ts, xg=xg,
                             st=(hc == 0), sp=(hc == NHC - 1): e.matmul(
                        pb[ts][:, :],
                        xg[:, hcl * 512 + ts * 128:hcl * 512 + (ts + 1) * 128],
                        wv_s[:, hc * 512:(hc + 1) * 512], start=st, stop=sp),
                      reads=R(xg.name, wv_s.name), writes=[PS[ts]])
        for ts in range(4):
            gts = tt * 4 + ts
            ob = outb[ts]
            E("s", lambda e, ts=ts, ob=ob: e.activation(ob[:, :], pb[ts][:, :], AF.Copy),
              reads=[PS[ts]], writes=R(ob.name))
            D("g", v_st[0:HPC, :, gts * 128:(gts + 1) * 128].transpose([1, 0, 2]),
              ob.ap().rearrange("p (h d) -> p h d", h=HPC),
              reads=R(ob.name), writes=[("dr", "vst")])

    for hh in range(HPC):
        qb_, kb_, vb_, ab_ = qh[hh % 2], kh[hh % 2], vh[hh % 2], aoh[hh % 2]
        D("y", qb_[:, :], q_st[hh, :, :], reads=[("dr", f"qst{hh}")], writes=R(qb_.name))
        D("y", kb_[:, :], k_st[hh, :, :], reads=[("dr", f"kst{hh}")], writes=R(kb_.name))
        D("y", vb_[:, :], v_st[hh, :, :], reads=[("dr", "vst")], writes=R(vb_.name))
        for ti, tsr in ((0, qb_), (1, kb_)):
            for ch in range(4):
                E("v", lambda e, tsr=tsr, ch=ch: e.tensor_mul(
                    sqq[:, :], tsr[:, ch * 2048:(ch + 1) * 2048],
                    tsr[:, ch * 2048:(ch + 1) * 2048]),
                  reads=R(tsr.name), writes=R("sqq"))
                for sl4 in range(4):
                    bank = sl4 % 2
                    slot = ti * 16 + ch * 4 + sl4
                    E("t", lambda e, sl4=sl4, bank=bank: e.matmul(
                        pb[bank][0:1, :], ones_b[:, :],
                        sqq[:, sl4 * 512:(sl4 + 1) * 512], start=True, stop=True),
                      reads=R("sqq", "ones_b"), writes=[PS[bank]])
                    E("v", lambda e, slot=slot, bank=bank: e.reduce_max(
                        nparts[0:1, slot:slot + 1], pb[bank][0:1, :], axis=AX.X),
                      reads=[PS[bank]], writes=R("nparts"))
        E("v", lambda e: e.reduce_max(nrow3[0:1, 0:1], nparts[0:1, 0:16], axis=AX.X),
          reads=R("nparts"), writes=R("nrow3"))
        E("v", lambda e: e.reduce_max(nrow3[0:1, 1:2], nparts[0:1, 16:32], axis=AX.X),
          reads=R("nparts"), writes=R("nrow3"))
        E("v", lambda e: e.tensor_mul(nrow3[0:1, 2:3], nrow3[0:1, 0:1], nrow3[0:1, 1:2]),
          reads=R("nrow3"), writes=R("nrow3"))
        E("s", lambda e: e.activation(nrow3[0:1, 3:4], nrow3[0:1, 2:3], AF.Sqrt),
          reads=R("nrow3"), writes=R("nrow3"))
        E("v", lambda e: e.tensor_scalar_mul(nrow3[0:1, 3:4], nrow3[0:1, 3:4], -1.0),
          reads=R("nrow3"), writes=R("nrow3"))
        E("t", lambda e: e.matmul(pb[2][:, 0:1], ones_r[:, :], nrow3[0:1, 3:4],
                                  start=True, stop=True),
          reads=R("nrow3", "ones_r"), writes=[PS[2]])
        E("s", lambda e, hh=hh: e.copy(negc[:, hh:hh + 1], pb[2][:, 0:1]),
          reads=[PS[2]], writes=R("negc"))
        for qb in range(16):
            qs = slice(qb * 512, (qb + 1) * 512)
            npair = 2 * (qb + 1)

            def emit_S(p, hh=hh, kb_=kb_, qb_=qb_, qs=qs, npair=npair):
                sP = pp[p % 2]
                sB = [PS[2 * (p % 2)], PS[2 * (p % 2) + 1]]
                pt = ptb[p % 2]
                for j in range(2):
                    k0 = (2 * p + j) * 128
                    E("t", lambda e, sP=sP, j=j, kb_=kb_, qb_=qb_, k0=k0, qs=qs:
                        e.matmul(sP[:, j * 512:(j + 1) * 512],
                                 kb_[:, k0:k0 + 128], qb_[:, qs],
                                 start=True, stop=True),
                      reads=R(kb_.name, qb_.name), writes=[sB[j]])
                E("s", lambda e, pt=pt, sP=sP, hh=hh: e.activation(
                    pt[:, :], sP[:, :], AF.Exp, bias=negc[:, hh:hh + 1]),
                  reads=sB + R("negc"), writes=R(pt.name))
                if p >= npair - 2:
                    moff = (p - (npair - 2)) * 1024
                    E("v", lambda e, pt=pt, moff=moff: e.tensor_mul(
                        pt[:, :], pt[:, :], masks[:, moff:moff + 1024]),
                      reads=R(pt.name, "masks"), writes=R(pt.name))

            acc_b = 4 if qb % 2 == 0 else 6
            den_b = 5 if qb % 2 == 0 else 7

            def emit_PV(p, vb_=vb_, npair=npair, acc_b=acc_b, den_b=den_b):
                pt = ptb[p % 2]
                first, last = (p == 0), (p == npair - 1)
                for j in range(2):
                    k0 = (2 * p + j) * 128
                    E("t", lambda e, vb_=vb_, pt=pt, k0=k0, j=j, acc_b=acc_b,
                             st=(first and j == 0), sp=(last and j == 1): e.matmul(
                        pb[acc_b][:, :], vb_[:, k0:k0 + 128],
                        pt[:, j * 512:(j + 1) * 512], start=st, stop=sp),
                      reads=R(vb_.name, pt.name), writes=[PS[acc_b]])
                for j in range(2):
                    E("t", lambda e, pt=pt, j=j, den_b=den_b,
                             st=(first and j == 0), sp=(last and j == 1): e.matmul(
                        pb[den_b][0:1, :], ones_b[:, :],
                        pt[:, j * 512:(j + 1) * 512], start=st, stop=sp),
                      reads=R(pt.name, "ones_b"), writes=[PS[den_b]])

            emit_S(0)
            for p in range(npair):
                if p + 1 < npair:
                    emit_S(p + 1)
                emit_PV(p)
            rrow = nrow if qb % 2 == 0 else nrow2
            E("v", lambda e, den_b=den_b, rrow=rrow: e.reciprocal(
                rrow[0:1, 0:512], pb[den_b][0:1, :]),
              reads=[PS[den_b]], writes=R(rrow.name))
            E("t", lambda e, den_b=den_b, rrow=rrow: e.matmul(
                pb[den_b][:, :], ones_r[:, :], rrow[0:1, 0:512],
                start=True, stop=True),
              reads=R(rrow.name, "ones_r"), writes=[PS[den_b]])
            bc = bcs[qb % 2]
            E("s", lambda e, bc=bc, den_b=den_b: e.copy(bc[:, :], pb[den_b][:, :]),
              reads=[PS[den_b]], writes=R(bc.name))
            E("v", lambda e, ab_=ab_, qs=qs, bc=bc, acc_b=acc_b: e.tensor_mul(
                ab_[:, qs], pb[acc_b][:, :], bc[:, :]),
              reads=[PS[acc_b]] + R(bc.name), writes=R(ab_.name))
        D("g", a_pre.rearrange("(j f) t -> f j t", j=NC)[hh * 128:(hh + 1) * 128, :, :],
          ab_.ap().rearrange("p (j t) -> p j t", j=NC),
          reads=R(ab_.name), writes=[("dr", "a_pre")])

    E("g", lambda e: e.dma_start(out=a_st, in_=a_pre),
      reads=[("dr", "a_pre")], writes=[("dr", "a_st")], kind="c")
    D("g", zB[:, 0:HS], u1_r[0], reads=[("dr", "u1_r")], writes=R("zB"))
    D("g", zB[:, HS:2 * HS], u1_r[1], reads=[("dr", "u1_r")], writes=R("zB"))
    for i in (0, 1):
        D("g", v1g[i][:, :], v1_r[i], reads=[("dr", "v1_r")], writes=R(v1g[i].name))
        D("g", v1u[i][:, :], v1_r[NFC + i], reads=[("dr", "v1_r")], writes=R(v1u[i].name))
    for gi in range(G_FFN):
        D("g", u2g[gi][:, :], u2_r[gi], reads=[("dr", "u2_r")], writes=R(u2g[gi].name))
    E("g", lambda e: e.collective_compute(
        "AllGather", ALU.bypass, replica_groups=[list(range(NC))],
        ins=[a_st], outs=[a_g]),
      reads=[("dr", "a_st")], writes=[("dr", "a_g")], kind="cc")
    E("g", lambda e: e.collective_compute(
        "AllReduce", ALU.add, replica_groups=[list(range(NC))],
        ins=[bar2i], outs=[bar2o]),
      reads=[("dr", "a_g")], writes=[("dr", "a_g"), ("dr", "bar2")], kind="cc")

    E("v", lambda e: e.memset(zS[:, :], 0.0), writes=R("zS"))
    for half2 in range(2):
        D("y", xno.ap()[:, half2 * 16 * 1024:(half2 + 1) * 16 * 1024],
          xn_st[half2 * 2048:(half2 + 1) * 2048, :].rearrange("(hc p) t -> p hc t", p=128),
          reads=[("dr", "xn_st")], writes=R("xno"))
    for rc in range(RC):
        ub = zB.ap()[:, (rc % 2) * HS:(rc % 2 + 1) * HS]
        if rc > 1:
            D("g", ub, u1_r[rc], reads=[("dr", "u1_r")], writes=R("zB"))
        for half in range(2):
            bank = 2 * (rc % 2) + half
            for hc in range(NHC):
                E("t", lambda e, ub=ub, hc=hc, half=half, bank=bank,
                         st=(hc == 0), sp=(hc == NHC - 1): e.matmul(
                    pb[bank][:, :], ub[:, hc * 128:(hc + 1) * 128],
                    xno[:, hc * 1024 + half * 512:hc * 1024 + (half + 1) * 512],
                    start=st, stop=sp),
                  reads=R("zB", "xno"), writes=[PS[bank]])
            E("s", lambda e, rc=rc, half=half, bank=bank: e.copy(
                p1[:, rc * 1024 + half * 512:rc * 1024 + (half + 1) * 512],
                pb[bank][:, :]),
              reads=[PS[bank]], writes=R("p1"))
    fgroups = []
    i0 = 0
    while i0 < NFC:
        fgroups.append(list(range(i0, min(i0 + G_FFN, NFC))))
        i0 += G_FFN
    for grp in fgroups:
        for gi, i in enumerate(grp):
            vg, vu = v1g[i % 2], v1u[i % 2]
            ag = actg[gi]
            if i > 1:
                D("g", vg[:, :], v1_r[i], reads=[("dr", "v1_r")], writes=R(vg.name))
                D("g", vu[:, :], v1_r[NFC + i], reads=[("dr", "v1_r")], writes=R(vu.name))
            if i >= G_FFN:
                D("g", u2g[gi][:, :], u2_r[i], reads=[("dr", "u2_r")],
                  writes=R(u2g[gi].name))
            for half in range(2):
                gb, ub_ = half, 2 + half
                for rc in range(RC):
                    E("t", lambda e, vg=vg, rc=rc, half=half, gb=gb,
                             st=(rc == 0), sp=(rc == RC - 1): e.matmul(
                        pb[gb][:, :], vg[:, rc * 128:(rc + 1) * 128],
                        p1[:, rc * 1024 + half * 512:rc * 1024 + (half + 1) * 512],
                        start=st, stop=sp),
                      reads=R(vg.name, "p1"), writes=[PS[gb]])
                    E("t", lambda e, vu=vu, rc=rc, half=half, ub_=ub_,
                             st=(rc == 0), sp=(rc == RC - 1): e.matmul(
                        pb[ub_][:, :], vu[:, rc * 128:(rc + 1) * 128],
                        p1[:, rc * 1024 + half * 512:rc * 1024 + (half + 1) * 512],
                        start=st, stop=sp),
                      reads=R(vu.name, "p1"), writes=[PS[ub_]])
                E("s", lambda e, half=half, gb=gb, i=i: e.activation(
                    silub[:, half * 512:(half + 1) * 512], pb[gb][:, :], AF.Silu,
                    bias=b1s[:, i:i + 1]),
                  reads=[PS[gb]] + R("b1s"), writes=R("silub"))
                E("v", lambda e, ag=ag, half=half, ub_=ub_, i=i: e.scalar_tensor_tensor(
                    ag[:, half * 512:(half + 1) * 512], pb[ub_][:, :],
                    b1s[:, NFC + i:NFC + i + 1],
                    silub[:, half * 512:(half + 1) * 512], ALU.add, ALU.mult),
                  reads=[PS[ub_]] + R("b1s", "silub"), writes=R(ag.name))
        for rc in range(RC):
            for half in range(2):
                bank = 4 + half
                for gi, i in enumerate(grp):
                    ag = actg[gi]
                    ub2 = u2g[gi]
                    E("t", lambda e, ub2=ub2, ag=ag, rc=rc, half=half, bank=bank,
                             st=(gi == 0), sp=(gi == len(grp) - 1): e.matmul(
                        pb[bank][:, :], ub2[:, rc * 128:(rc + 1) * 128],
                        ag[:, half * 512:(half + 1) * 512], start=st, stop=sp),
                      reads=R(ub2.name, ag.name), writes=[PS[bank]])
                E("v", lambda e, rc=rc, half=half, bank=bank: e.tensor_add(
                    zS[:, rc * 1024 + half * 512:rc * 1024 + (half + 1) * 512],
                    zS[:, rc * 1024 + half * 512:rc * 1024 + (half + 1) * 512],
                    pb[bank][:, :]),
                  reads=[PS[bank]] + R("zS"), writes=R("zS"))
    E("v", lambda e: e.tensor_copy(zB[:, :], zS[:, :]), reads=R("zS"), writes=R("zB"))

    for hi in range(NHC):
        E("g", lambda e, hi=hi: e.indirect_dma_start(
            aslab[:, hi * 1024:(hi + 1) * 1024], None, a_g,
            IndirectOffsetOnAxis(idxs[:, hi:hi + 1], 0)),
          reads=[("dr", "a_g")] + R("idxs"), writes=R("aslab"), kind="d")
    for ho in range(NHC):
        wb = wob[ho % 2]
        v2 = v2b[ho % 2]
        xc = xfc[ho % 2]
        oS = outS[ho % 2]
        D("g", wb[:, :], wo_r[ho], reads=[("dr", "wo_r")], writes=R(wb.name))
        D("g", v2[:, :], v2_r[ho], reads=[("dr", "v2_r")], writes=R(v2.name))
        D("y", xc[:, :], xf[ho * 128:(ho + 1) * 128, :],
          reads=[("dr", "xf")], writes=R(xc.name))
        for half in range(2):
            bank = 2 * (ho % 2) + half
            for hi in range(NHC):
                E("t", lambda e, wb=wb, hi=hi, half=half, bank=bank,
                         st=(hi == 0): e.matmul(
                    pb[bank][:, :], wb[:, hi * 128:(hi + 1) * 128],
                    aslab[:, hi * 1024 + half * 512:hi * 1024 + (half + 1) * 512],
                    start=st, stop=False),
                  reads=R(wb.name, "aslab"), writes=[PS[bank]])
            for rc in range(RC):
                E("t", lambda e, v2=v2, rc=rc, half=half, bank=bank,
                         sp=(rc == RC - 1): e.matmul(
                    pb[bank][:, :], v2[:, rc * 128:(rc + 1) * 128],
                    zB[:, rc * 1024 + half * 512:rc * 1024 + (half + 1) * 512],
                    start=False, stop=sp),
                  reads=R(v2.name, "zB"), writes=[PS[bank]])
            E("v", lambda e, oS=oS, xc=xc, half=half, bank=bank, ho=ho: e.scalar_tensor_tensor(
                oS[:, half * 512:(half + 1) * 512], pb[bank][:, :],
                b2s[:, ho:ho + 1], xc[:, half * 512:(half + 1) * 512],
                ALU.add, ALU.add),
              reads=[PS[bank]] + R(xc.name, "b2s"), writes=R(oS.name))
        D("y", out_t[ho * 128:(ho + 1) * 128, :], oS[:, :],
          reads=R(oS.name), writes=[("dr", "out")])
    if dbg:
        for nm, src_ap in (("xn_st", xn_st), ("xn_g", xn_g), ("q_st", q_st),
                           ("k_st", k_st), ("v_st", v_st), ("a_st", a_st),
                           ("a_g", a_g)):
            D("g", dbg_t[nm], src_ap, reads=[("dr", nm.replace("q_st", "qst").replace("k_st", "kst").replace("v_st", "vst"))],
              writes=[("dr", "d" + nm)])
    dbg_reads = [("dr", "d" + nm) for nm in dbg_t] if dbg else []
    E("y", None, reads=[("dr", "out")] + dbg_reads)

    b.finalize()
    return nc




_cache = {}


def _host_prep():
    if "c" in _cache:
        return _cache["c"]
    inv_freq = 1.0 / 10000.0 ** (np.arange(0, HD, 2, dtype=np.float32) / HD)
    pos = np.arange(SL, dtype=np.float32)
    emb = np.concatenate([pos[:, None] * inv_freq[None, :]] * 2, -1)
    cos_t = np.ascontiguousarray(np.cos(emb).astype(np.float32).T)
    sin_t = np.ascontiguousarray(np.sin(emb).astype(np.float32).T)
    m = np.zeros((128, 4 * 512), np.float32)
    for kt in range(4):
        m[:, kt * 512:(kt + 1) * 512] = (
            kt * 128 + np.arange(128)[:, None] <= np.arange(512)[None, :])
    mask_t = m.astype(BF)
    idxs = []
    for c in range(NC):
        idx = np.zeros((128, NHC), np.int32)
        for hi in range(NHC):
            r, hh = hi // HPC, hi % HPC
            idx[:, hi] = r * HS + c * 512 + hh * 128 + np.arange(128)
        idxs.append(idx)
    _cache["c"] = (cos_t, sin_t, mask_t, idxs)
    return _cache["c"]


def kernel(L0_attn_hidden_states, L0_attn_wq, L0_attn_wk, L0_attn_wv, L0_attn_wo,
           L0_ffn_U1, L0_ffn_V1, L0_ffn_U2, L0_ffn_V2, L0_ffn_b1, L0_ffn_b2):
    x = np.asarray(L0_attn_hidden_states, np.float32).reshape(SL, HS)
    wq = np.asarray(L0_attn_wq, np.float32)
    wk = np.asarray(L0_attn_wk, np.float32)
    wv = np.asarray(L0_attn_wv, np.float32)
    wo = np.asarray(L0_attn_wo, np.float32)
    U1 = np.asarray(L0_ffn_U1, np.float32)
    V1 = np.asarray(L0_ffn_V1, np.float32)
    U2 = np.asarray(L0_ffn_U2, np.float32)
    V2 = np.asarray(L0_ffn_V2, np.float32)
    b1 = np.asarray(L0_ffn_b1, np.float32)
    b2 = np.asarray(L0_ffn_b2, np.float32)

    cos_t, sin_t, mask_t, idxs = _host_prep()
    wo_r = np.ascontiguousarray(
        wo.reshape(NHC, 128, NHC, 128).transpose(2, 1, 0, 3).reshape(NHC, 128, HS))
    u1_r = np.ascontiguousarray(
        U1.reshape(NHC, 128, RC, 128).transpose(2, 1, 0, 3).reshape(RC, 128, HS))
    v1_r = np.ascontiguousarray(
        V1.reshape(RC, 128, 2 * NFC, 128).transpose(2, 1, 0, 3).reshape(2 * NFC, 128, RANK))
    u2_r = np.ascontiguousarray(U2.reshape(NFC, 128, RANK))
    v2_r = np.ascontiguousarray(
        V2.reshape(RC, 128, NHC, 128).transpose(2, 1, 0, 3).reshape(NHC, 128, RANK))
    b1_t = np.ascontiguousarray(b1.reshape(2 * NFC, 128).T)
    b2_t = np.ascontiguousarray(b2.reshape(NHC, 128).T)

    nc = build_nc()
    in_maps = []
    for c in range(NC):
        hsl = slice(c * HPC * HD, (c + 1) * HPC * HD)
        in_maps.append(dict(
            xf=np.ascontiguousarray(x[c * SLAB:(c + 1) * SLAB].T),
            wq=np.ascontiguousarray(wq[:, hsl]),
            wk=np.ascontiguousarray(wk[:, hsl]),
            wv=np.ascontiguousarray(wv[:, hsl]),
            wo_r=wo_r, u1_r=u1_r, v1_r=v1_r, u2_r=u2_r, v2_r=v2_r,
            b1_t=b1_t, b2_t=b2_t, cos_t=cos_t, sin_t=sin_t,
            mask_t=mask_t, idx_t=idxs[c]))
    res = run_bass_kernel_spmd(nc, in_maps, core_ids=list(range(NC)))
    out = np.concatenate([res.results[c]["out"].T for c in range(NC)], 0)
    return out.reshape(1, SL, HS)


# version 10
# speedup vs baseline: 1.1447x; 1.0038x over previous
import numpy as np
import ml_dtypes
import concourse.bass as bass
import concourse.mybir as mybir
from concourse.bass import IndirectOffsetOnAxis
from concourse.bass_utils import run_bass_kernel_spmd

F32 = mybir.dt.float32
BF16 = mybir.dt.bfloat16
I32 = mybir.dt.int32
AF = mybir.ActivationFunctionType
ALU = mybir.AluOpType
AX = mybir.AxisListType

HS, SL, HD, NH, NC = 4096, 8192, 128, 32, 8
HPC, SLAB = NH // NC, SL // NC
INTER, RANK = 11008, 1024
NFC = INTER // HD
NHC = HS // HD
RC = RANK // HD
QSC = float(1.0 / np.sqrt(HD))
G_FFN = 7

BF = ml_dtypes.bfloat16




class B:

    DMAQ = ("g", "y")

    def __init__(self, nc):
        self.nc = nc
        self.ops = []
        self.sem = {e: nc.alloc_semaphore(f"ctr_{e}") for e in "tvsgy"}

    def emit(self, eng, fn, reads=(), writes=(), kind="c", tag=""):
        self.ops.append(dict(eng=eng, fn=fn, reads=tuple(reads), writes=tuple(writes),
                             kind=kind, tag=tag, i=len(self.ops), need=False,
                             val=None, deps=[]))

    def dma(self, eng, out_ap, in_ap, reads, writes, tag=""):
        self.emit(eng, lambda e, o=out_ap, i=in_ap: e.dma_start(out=o, in_=i),
                  reads=reads, writes=writes, kind="d", tag=tag)

    def plan(self):
        last_w, readers = {}, {}
        for op in self.ops:
            deps = set()
            for r in op["reads"]:
                w = last_w.get(r)
                if w is not None:
                    deps.add(w)
            for w in op["writes"]:
                lw = last_w.get(w)
                if lw is not None:
                    deps.add(lw)
                deps.update(readers.get(w, ()))
            for r in op["reads"]:
                readers.setdefault(r, set()).add(op["i"])
            for w in op["writes"]:
                last_w[w] = op["i"]
                readers[w] = set()
            deps.discard(op["i"])
            rd = []
            for d in deps:
                p = self.ops[d]
                if p["eng"] == op["eng"] and op["eng"] == "t":
                    continue
                rd.append(d)
            op["deps"] = rd
            for d in rd:
                self.ops[d]["need"] = True
            if op["kind"] in ("d", "cc"):
                op["need"] = True
        cum = {e: 0 for e in "tvsgy"}
        for op in self.ops:
            if op["need"]:
                inc = 1 if (op["eng"] in "tvs" or op["kind"] == "cc") else 16
                cum[op["eng"]] += inc
                op["val"] = cum[op["eng"]]
                op["inc"] = inc
        seen = {c: {p: 0 for p in "tvsgy"} for c in "tvsgy"}
        for op in self.ops:
            req = {}
            for d in op["deps"]:
                p = self.ops[d]
                v = p["val"]
                req[p["eng"]] = max(req.get(p["eng"], 0), v)
            s = seen[op["eng"]]
            op["waits"] = [(e, v) for e, v in req.items() if v > s[e]]
            for e, v in req.items():
                s[e] = max(s[e], v)

    def replay(self, ename, eng):
        for op in self.ops:
            if op["eng"] != ename:
                continue
            for (e, v) in op["waits"]:
                eng.wait_ge(self.sem[e], v)
            if op["fn"] is None:
                continue
            inst = op["fn"](eng)
            if op["need"]:
                inst.then_inc(self.sem[ename], op["inc"])

    def finalize(self):
        self.plan()
        nc = self.nc
        with nc.Block() as block:
            block.tensor(lambda e: self.replay("t", e))
            block.vector(lambda e: self.replay("v", e))
            block.scalar(lambda e: self.replay("s", e))
            block.gpsimd(lambda e: self.replay("g", e))
            block.sync(lambda e: self.replay("y", e))


def build_nc(dbg=False):
    nc = bass.Bass("TRN2", target_bir_lowering=False)

    d_in = {}
    def din(name, shape, dt=F32):
        d_in[name] = nc.dram_tensor(name, shape, dt, kind="ExternalInput").ap()
        return d_in[name]

    xf = din("xf", [HS, SLAB])
    wq = din("wq", [HS, HPC * HD])
    wk = din("wk", [HS, HPC * HD])
    wv = din("wv", [HS, HPC * HD])
    wo_r = din("wo_r", [NHC, 128, HS])
    u1_r = din("u1_r", [RC, 128, HS])
    v1_r = din("v1_r", [2 * NFC, 128, RANK])
    u2_r = din("u2_r", [NFC, 128, RANK])
    v2_r = din("v2_r", [NHC, 128, RANK])
    b1_t = din("b1_t", [128, 2 * NFC])
    b2_t = din("b2_t", [128, NHC])
    cos_t = din("cos_t", [128, SL])
    sin_t = din("sin_t", [128, SL])
    mask_t = din("mask_t", [128, 4 * 512], BF16)
    idx_t = din("idx_t", [128, NHC], I32)

    xn_pre = nc.dram_tensor("xn_pre", [HS, SLAB], BF16, kind="Internal").ap()
    xn_st = nc.dram_tensor("xn_st", [HS, SLAB], BF16, kind="Internal").ap()
    xn_g = nc.dram_tensor("xn_g", [NC * HS, SLAB], BF16, kind="Internal",
                          addr_space="Shared").ap()
    q_st = nc.dram_tensor("q_st", [HPC, 128, SL], BF16, kind="Internal").ap()
    k_st = nc.dram_tensor("k_st", [HPC, 128, SL], BF16, kind="Internal").ap()
    v_st = nc.dram_tensor("v_st", [HPC, 128, SL], BF16, kind="Internal").ap()
    a_pre = nc.dram_tensor("a_pre", [HS, SLAB], BF16, kind="Internal").ap()
    a_st = nc.dram_tensor("a_st", [HS, SLAB], BF16, kind="Internal").ap()
    a_g = nc.dram_tensor("a_g", [NC * HS, SLAB], BF16, kind="Internal",
                         addr_space="Shared").ap()
    out_t = nc.dram_tensor("out", [HS, SLAB], F32, kind="ExternalOutput").ap()
    bar1i = nc.dram_tensor("bar1i", [1, 128], F32, kind="Internal").ap()
    bar1o = nc.dram_tensor("bar1o", [1, 128], F32, kind="Internal").ap()
    bar2i = nc.dram_tensor("bar2i", [1, 128], F32, kind="Internal").ap()
    aslab_d = nc.dram_tensor("aslab_d", [128, NHC * 1024], BF16, kind="Internal").ap()
    bar2o = nc.dram_tensor("bar2o", [1, 128], F32, kind="Internal").ap()
    dbg_t = {}
    if dbg:
        for nm, ap_ in (("xn_st", xn_st), ("xn_g", xn_g), ("q_st", q_st),
                        ("k_st", k_st), ("v_st", v_st), ("a_st", a_st),
                        ("a_g", a_g)):
            dbg_t[nm] = nc.dram_tensor("d_" + nm, list(ap_.shape), BF16,
                                       kind="ExternalOutput").ap()

    BASE = ((nc.sbuf_base + 63) // 64) * 64
    TOP = nc.sbuf_top
    alloc_log = []

    def sb(name, shape, dt, off):
        a = nc.alloc_sbuf_tensor_at(name, shape, dt, offset=BASE + off)
        nbytes = int(np.prod(shape[1:])) * mybir.dt.size(dt)
        assert BASE + off + nbytes <= TOP, (name, off, nbytes)
        alloc_log.append((name, off, nbytes, a.name))
        return a

    off = 0
    def bump(n):
        nonlocal off
        r = off
        off = ((off + n + 1023) // 1024) * 1024
        return r
    ones_b = sb("ones_b", [128, 1], BF16, bump(2))
    ones_c = sb("ones_c", [128, 1], F32, bump(4))
    ones_r = sb("ones_r", [1, 128], F32, bump(512))
    masks = sb("masks", [128, 2048], BF16, bump(4096))
    b1s = sb("b1s", [128, 2 * NFC], F32, bump(2 * NFC * 4))
    b2s = sb("b2s", [128, NHC], F32, bump(NHC * 4))
    negc = sb("negc", [128, HPC], F32, bump(HPC * 4))
    nrow = sb("nrow", [1, 1024], F32, bump(4096))
    nrow2 = sb("nrow2", [1, 1024], F32, bump(4096))
    nrow3 = sb("nrow3", [1, 1024], F32, bump(4096))
    nparts = sb("nparts", [1, 256], F32, bump(1024))
    epsb = sb("epsb", [1, 1], F32, bump(4))
    idxs = sb("idxs", [128, NHC], I32, bump(NHC * 4))
    PERSIST = ((off + 1023) // 1024) * 1024

    def region(*specs):
        specs = list(specs)
        o = PERSIST
        out = []
        if specs and isinstance(specs[0], int):
            o = PERSIST + specs[0]
            specs = specs[1:]
        for (name, shape, dt) in specs:
            nbytes = int(np.prod(shape[1:])) * mybir.dt.size(dt)
            out.append(sb(name, shape, dt, o))
            o = ((o + nbytes + 63) // 64) * 64
        assert BASE + o <= TOP, ("region overflow", o)
        return out

    xch0, xch1, sqch, bc_r, bc_b, xnb0, xnb1 = region(
        ("xch0", [128, 1024], F32), ("xch1", [128, 1024], F32),
        ("sqch", [128, 1024], F32),
        ("bc_r", [128, 1024], F32), ("bc_b", [128, 1024], F32),
        ("xnb0", [128, 1024], BF16), ("xnb1", [128, 1024], BF16))
    xch = [xch0, xch1]
    xnb = [xnb0, xnb1]

    (wq_s, wk_s, cosb0, cosb1, sinb0, sinb1, xng0, xng1, qf32, rotb, t1b, sqb,
     outb0, outb1, outb2, outb3) = region(
        ("wq_s", [128, NHC * 512], BF16), ("wk_s", [128, NHC * 512], BF16),
        ("cosb0", [128, 512], F32), ("cosb1", [128, 512], F32),
        ("sinb0", [128, 512], F32), ("sinb1", [128, 512], F32),
        ("xng0", [128, 8 * 512], BF16), ("xng1", [128, 8 * 512], BF16),
        ("qf32", [128, 512], F32), ("rotb", [128, 512], F32),
        ("t1b", [128, 512], F32), ("sqb", [128, 512], BF16),
        ("outb0", [128, 512], BF16), ("outb1", [128, 512], BF16),
        ("outb2", [128, 512], BF16), ("outb3", [128, 512], BF16))
    cosb, sinb = [cosb0, cosb1], [sinb0, sinb1]
    xng = [xng0, xng1]
    outb = [outb0, outb1, outb2, outb3]
    wv_s = wk_s

    (qh0, qh1, kh0, kh1, vh0, vh1, aoh0, aoh1, ptb0, ptb1, bcs0, bcs1) = region(
        ("qh0", [128, SL], BF16), ("qh1", [128, SL], BF16),
        ("kh0", [128, SL], BF16), ("kh1", [128, SL], BF16),
        ("vh0", [128, SL], BF16), ("vh1", [128, SL], BF16),
        ("aoh0", [128, SL], BF16), ("aoh1", [128, SL], BF16),
        ("ptb0", [128, 1024], BF16), ("ptb1", [128, 1024], BF16),
        ("bcs0", [128, 512], F32), ("bcs1", [128, 512], F32))
    qh, kh, vh, aoh = [qh0, qh1], [kh0, kh1], [vh0, vh1], [aoh0, aoh1]
    ptb, bcs = [ptb0, ptb1], [bcs0, bcs1]
    (sqq,) = region(137 * 1024, ("sqq", [128, 2048], BF16))

    ffn_specs = [("xno", [128, NHC * 1024], BF16),
                 ("p1", [128, RC * 1024], BF16),
                 ("zS", [128, RC * 1024], F32)]
    ffn_specs += [(f"actg{i}", [128, 1024], BF16) for i in range(G_FFN)]
    ffn_t = region(*ffn_specs)
    (xno, p1, zS) = ffn_t[:3]
    actg = ffn_t[3:]
    ffn_s = [141 * 1024,
             ("u1b0", [128, HS], BF16),
             ("v1g0", [128, 1024], BF16), ("v1g1", [128, 1024], BF16),
             ("v1u0", [128, 1024], BF16), ("v1u1", [128, 1024], BF16),
             ("silub", [128, 1024], F32)]
    ffn_s += [(f"u2g{i}", [128, 1024], BF16) for i in range(G_FFN)]
    ffn_st = region(*ffn_s)
    (u1b0, v1g0, v1g1, v1u0, v1u1, silub) = ffn_st[:6]
    u2g = ffn_st[6:6 + G_FFN]
    (zB,) = region(126 * 1024, ("zB", [128, RC * 1024], BF16))
    u1b = [u1b0, u1b0]
    v1g, v1u = [v1g0, v1g1], [v1u0, v1u1]

    (aslab, wob0, wob1, v2b0, v2b1, xfc0, xfc1, outS0, outS1) = region(
        ("aslab", [128, NHC * 1024], BF16),
        ("wob0", [128, HS], BF16), ("wob1", [128, HS], BF16),
        ("v2b0", [128, 1024], BF16), ("v2b1", [128, 1024], BF16),
        ("xfc0", [128, 1024], F32), ("xfc1", [128, 1024], F32),
        ("outS0", [128, 1024], F32), ("outS1", [128, 1024], F32))
    wob, v2b = [wob0, wob1], [v2b0, v2b1]
    xfc, outS = [xfc0, xfc1], [outS0, outS1]

    offs = {n: o for (n, o, _, _) in alloc_log}
    zb_off = offs["zB"]
    fin_end = offs["outS1"] + 4096
    assert fin_end <= zb_off, (fin_end, zb_off)

    pages = {}
    for (name, o, nb, real) in alloc_log:
        pg = tuple(("P", p) for p in range(o // 1024, (o + nb + 1023) // 1024))
        pages[name] = pg
        pages[real] = pg

    def R(*ts):
        out = []
        for t in ts:
            out.extend(pages[t if isinstance(t, str) else t.name])
        return out

    pp = [nc.alloc_psum_tensor(f"pp{i}", [128, 1024], F32) for i in range(4)]
    pb = [pp[i // 2].ap()[:, (i % 2) * 512:(i % 2 + 1) * 512] for i in range(8)]
    PS = [("ps", i) for i in range(8)]

    b = B(nc)
    E, D = b.emit, b.dma

    E("v", lambda e: e.memset(ones_b[:, :], 1.0), writes=R("ones_b"))
    E("v", lambda e: e.memset(ones_c[:, :], 1.0), writes=R("ones_c"))
    E("v", lambda e: e.memset(ones_r[:, :], 1.0), writes=R("ones_r"))
    E("v", lambda e: e.memset(epsb[:, :], 1e-5), writes=R("epsb"))
    D("y", masks[:, :], mask_t, reads=[("dr", "mask_t")], writes=R("masks"))
    D("y", b1s[:, :], b1_t, reads=[("dr", "b1_t")], writes=R("b1s"))
    D("y", b2s[:, :], b2_t, reads=[("dr", "b2_t")], writes=R("b2s"))
    D("y", idxs[:, :], idx_t, reads=[("dr", "idx_t")], writes=R("idxs"))

    for ch in range(NHC):
        c = xch[ch % 2]
        D("y", c[:, :], xf[ch * 128:(ch + 1) * 128, :],
          reads=[("dr", "xf")], writes=R(c.name))
        E("s", lambda e, c=c: e.activation(sqch[:, :], c[:, :], AF.Square),
          reads=R(c.name), writes=R("sqch"))
        st, sp = (ch == 0), (ch == NHC - 1)
        for half in range(2):
            E("t", lambda e, c=c, h=half, st=st, sp=sp: e.matmul(
                pb[h][0:1, :], ones_c[:, :], c[:, h * 512:(h + 1) * 512],
                start=st, stop=sp),
              reads=R(c.name, "ones_c"), writes=[PS[half]])
            E("t", lambda e, h=half, st=st, sp=sp: e.matmul(
                pb[2 + h][0:1, :], ones_c[:, :], sqch[:, h * 512:(h + 1) * 512],
                start=st, stop=sp),
              reads=R("sqch", "ones_c"), writes=[PS[2 + half]])
    for half in range(2):
        E("v", lambda e, h=half: e.tensor_scalar_mul(
            nrow[0:1, h * 512:(h + 1) * 512], pb[h][0:1, :], 1.0 / HS),
          reads=[PS[half]], writes=R("nrow"))
        E("v", lambda e, h=half: e.tensor_scalar_mul(
            nrow2[0:1, h * 512:(h + 1) * 512], pb[2 + h][0:1, :], 1.0 / HS),
          reads=[PS[2 + half]], writes=R("nrow2"))
    E("v", lambda e: e.tensor_mul(nrow3[0:1, :], nrow[0:1, :], nrow[0:1, :]),
      reads=R("nrow"), writes=R("nrow3"))
    E("v", lambda e: e.tensor_sub(nrow2[0:1, :], nrow2[0:1, :], nrow3[0:1, :]),
      reads=R("nrow2", "nrow3"), writes=R("nrow2"))
    E("s", lambda e: e.activation(nrow3[0:1, :], nrow2[0:1, :], AF.Sqrt,
                                  bias=epsb[0:1, 0:1]),
      reads=R("nrow2", "epsb"), writes=R("nrow3"))
    E("v", lambda e: e.reciprocal(nrow2[0:1, :], nrow3[0:1, :]),
      reads=R("nrow3"), writes=R("nrow2"))
    E("v", lambda e: e.tensor_mul(nrow3[0:1, :], nrow[0:1, :], nrow2[0:1, :]),
      reads=R("nrow", "nrow2"), writes=R("nrow3"))
    E("v", lambda e: e.tensor_scalar_mul(nrow3[0:1, :], nrow3[0:1, :], -1.0),
      reads=R("nrow3"), writes=R("nrow3"))
    for half in range(2):
        E("t", lambda e, h=half: e.matmul(
            pb[4 + h][:, :], ones_r[:, :], nrow2[0:1, h * 512:(h + 1) * 512],
            start=True, stop=True),
          reads=R("nrow2", "ones_r"), writes=[PS[4 + half]])
        E("t", lambda e, h=half: e.matmul(
            pb[6 + h][:, :], ones_r[:, :], nrow3[0:1, h * 512:(h + 1) * 512],
            start=True, stop=True),
          reads=R("nrow3", "ones_r"), writes=[PS[6 + half]])
        E("s", lambda e, h=half: e.copy(bc_r[:, h * 512:(h + 1) * 512], pb[4 + h][:, :]),
          reads=[PS[4 + half]], writes=R("bc_r"))
        E("s", lambda e, h=half: e.copy(bc_b[:, h * 512:(h + 1) * 512], pb[6 + h][:, :]),
          reads=[PS[6 + half]], writes=R("bc_b"))
    for ch in range(NHC):
        c = xch[ch % 2]
        o = xnb[ch % 2]
        D("y", c[:, :], xf[ch * 128:(ch + 1) * 128, :],
          reads=[("dr", "xf")], writes=R(c.name))
        E("v", lambda e, c=c: e.tensor_mul(sqch[:, :], c[:, :], bc_r[:, :]),
          reads=R(c.name, "bc_r"), writes=R("sqch"))
        E("v", lambda e, o=o: e.tensor_add(o[:, :], sqch[:, :], bc_b[:, :]),
          reads=R("sqch", "bc_b"), writes=R(o.name))
        D("g", xn_pre[ch * 128:(ch + 1) * 128, :], o[:, :],
          reads=R(o.name), writes=[("dr", "xn_pre")])

    E("g", lambda e: e.dma_start(out=xn_st, in_=xn_pre),
      reads=[("dr", "xn_pre")], writes=[("dr", "xn_st")], kind="c")
    D("g", wq_s[:, :], wq.rearrange("(hc p) d -> p hc d", p=128),
      reads=[("dr", "wq")], writes=R("wq_s"))
    D("g", wk_s[:, :], wk.rearrange("(hc p) d -> p hc d", p=128),
      reads=[("dr", "wk")], writes=R("wk_s"))
    E("g", lambda e: e.collective_compute(
        "AllGather", ALU.bypass, replica_groups=[list(range(NC))],
        ins=[xn_st], outs=[xn_g]),
      reads=[("dr", "xn_st")], writes=[("dr", "xn_g")], kind="cc")
    E("g", lambda e: e.collective_compute(
        "AllReduce", ALU.add, replica_groups=[list(range(NC))],
        ins=[bar1i], outs=[bar1o]),
      reads=[("dr", "xn_g")], writes=[("dr", "xn_g"), ("dr", "bar1")], kind="cc")


    def qk_evac(mat, hh, tt, bank):
        wsl = ("q" if mat == 0 else "k")
        st = q_st if mat == 0 else k_st
        scale = QSC if mat == 0 else 1.0
        ob = outb[(hh + 4 * (mat)) % 4]
        cb, sbn = cosb[tt % 2], sinb[tt % 2]
        E("s", lambda e: e.activation(qf32[:, :], pb[bank][:, :], AF.Copy, scale=scale),
          reads=[PS[bank]], writes=R("qf32"))
        E("v", lambda e: e.tensor_scalar_mul(rotb[0:64, :], qf32[64:128, :], -1.0),
          reads=R("qf32"), writes=R("rotb"))
        E("v", lambda e: e.tensor_copy(rotb[64:128, :], qf32[0:64, :]),
          reads=R("qf32"), writes=R("rotb"))
        E("v", lambda e, cb=cb: e.tensor_mul(t1b[:, :], qf32[:, :], cb[:, :]),
          reads=R("qf32", cb.name), writes=R("t1b"))
        E("v", lambda e, sbn=sbn: e.tensor_mul(rotb[:, :], rotb[:, :], sbn[:, :]),
          reads=R("rotb", sbn.name), writes=R("rotb"))
        E("v", lambda e, ob=ob: e.tensor_add(ob[:, :], t1b[:, :], rotb[:, :]),
          reads=R("t1b", "rotb"), writes=R(ob.name))
        D("g", st[hh, :, tt * 512:(tt + 1) * 512], ob[:, :],
          reads=R(ob.name), writes=[("dr", f"{wsl}st{hh}")])

    for tt in range(16):
        r, half = tt // 2, tt % 2
        D("y", cosb[tt % 2][:, :], cos_t[:, tt * 512:(tt + 1) * 512],
          reads=[("dr", "cos_t")], writes=R(cosb[tt % 2].name))
        D("y", sinb[tt % 2][:, :], sin_t[:, tt * 512:(tt + 1) * 512],
          reads=[("dr", "sin_t")], writes=R(sinb[tt % 2].name))
        for grp in range(4):
            xg = xng[grp % 2]
            src = xn_g[r * HS + grp * 1024:r * HS + (grp + 1) * 1024,
                       half * 512:(half + 1) * 512]
            D("y", xg[:, :], src.rearrange("(hc p) t -> p hc t", p=128),
              reads=[("dr", "xn_g")], writes=R(xg.name))
            for hcl in range(8):
                hc = grp * 8 + hcl
                rhs = xg[:, hcl * 512:(hcl + 1) * 512]
                for mat, ws in ((0, wq_s), (1, wk_s)):
                    for hh in range(HPC):
                        bank = mat * 4 + hh
                        E("t", lambda e, ws=ws, hc=hc, hh=hh, rhs=rhs, bank=bank,
                                 st=(hc == 0), sp=(hc == NHC - 1): e.matmul(
                            pb[bank][:, :], ws[:, hc * 512 + hh * 128:hc * 512 + (hh + 1) * 128],
                            rhs, start=st, stop=sp),
                          reads=R(ws.name, xg.name), writes=[PS[bank]])
        for mat in range(2):
            for hh in range(HPC):
                qk_evac(mat, hh, tt, mat * 4 + hh)

    D("g", wv_s[:, :], wv.rearrange("(hc p) d -> p hc d", p=128),
      reads=[("dr", "wv")], writes=R(wv_s.name))
    for tt in range(16):
        r, half = tt // 2, tt % 2
        for grp in range(4):
            xg = xng[grp % 2]
            src = xn_g[r * HS + grp * 1024:r * HS + (grp + 1) * 1024,
                       half * 512:(half + 1) * 512]
            D("y", xg[:, :], src.rearrange("(hc p) t -> p hc t", p=128),
              reads=[("dr", "xn_g")], writes=R(xg.name))
            for hcl in range(8):
                hc = grp * 8 + hcl
                for ts in range(4):
                    E("t", lambda e, hc=hc, hcl=hcl, ts=ts, xg=xg,
                             st=(hc == 0), sp=(hc == NHC - 1): e.matmul(
                        pb[ts][:, :],
                        xg[:, hcl * 512 + ts * 128:hcl * 512 + (ts + 1) * 128],
                        wv_s[:, hc * 512:(hc + 1) * 512], start=st, stop=sp),
                      reads=R(xg.name, wv_s.name), writes=[PS[ts]])
        for ts in range(4):
            gts = tt * 4 + ts
            ob = outb[ts]
            E("s", lambda e, ts=ts, ob=ob: e.activation(ob[:, :], pb[ts][:, :], AF.Copy),
              reads=[PS[ts]], writes=R(ob.name))
            D("g", v_st[0:HPC, :, gts * 128:(gts + 1) * 128].transpose([1, 0, 2]),
              ob.ap().rearrange("p (h d) -> p h d", h=HPC),
              reads=R(ob.name), writes=[("dr", "vst")])

    def head_load(hh):
        qb_, kb_, vb_ = qh[hh % 2], kh[hh % 2], vh[hh % 2]
        D("y", qb_[:, :], q_st[hh, :, :], reads=[("dr", f"qst{hh}")], writes=R(qb_.name))
        D("y", kb_[:, :], k_st[hh, :, :], reads=[("dr", f"kst{hh}")], writes=R(kb_.name))
        D("y", vb_[:, :], v_st[hh, :, :], reads=[("dr", "vst")], writes=R(vb_.name))

    def head_norm_unit(hh, u):
        ti, ch = u // 4, u % 4
        tsr = (qh if ti == 0 else kh)[hh % 2]
        E("v", lambda e, tsr=tsr, ch=ch: e.tensor_mul(
            sqq[:, :], tsr[:, ch * 2048:(ch + 1) * 2048],
            tsr[:, ch * 2048:(ch + 1) * 2048]),
          reads=R(tsr.name), writes=R("sqq"))
        for sl4 in range(4):
            bank = sl4 % 2
            slot = ti * 16 + ch * 4 + sl4
            E("t", lambda e, sl4=sl4, bank=bank: e.matmul(
                pb[bank][0:1, :], ones_b[:, :],
                sqq[:, sl4 * 512:(sl4 + 1) * 512], start=True, stop=True),
              reads=R("sqq", "ones_b"), writes=[PS[bank]])
            E("v", lambda e, slot=slot, bank=bank: e.reduce_max(
                nparts[0:1, slot:slot + 1], pb[bank][0:1, :], axis=AX.X),
              reads=[PS[bank]], writes=R("nparts"))

    def head_negc(hh):
        E("v", lambda e: e.reduce_max(nrow3[0:1, 0:1], nparts[0:1, 0:16], axis=AX.X),
          reads=R("nparts"), writes=R("nrow3"))
        E("v", lambda e: e.reduce_max(nrow3[0:1, 1:2], nparts[0:1, 16:32], axis=AX.X),
          reads=R("nparts"), writes=R("nrow3"))
        E("v", lambda e: e.tensor_mul(nrow3[0:1, 2:3], nrow3[0:1, 0:1], nrow3[0:1, 1:2]),
          reads=R("nrow3"), writes=R("nrow3"))
        E("s", lambda e: e.activation(nrow3[0:1, 3:4], nrow3[0:1, 2:3], AF.Sqrt),
          reads=R("nrow3"), writes=R("nrow3"))
        E("v", lambda e: e.tensor_scalar_mul(nrow3[0:1, 3:4], nrow3[0:1, 3:4], -1.0),
          reads=R("nrow3"), writes=R("nrow3"))
        E("t", lambda e: e.matmul(pb[2][:, 0:1], ones_r[:, :], nrow3[0:1, 3:4],
                                  start=True, stop=True),
          reads=R("nrow3", "ones_r"), writes=[PS[2]])
        E("s", lambda e, hh=hh: e.copy(negc[:, hh:hh + 1], pb[2][:, 0:1]),
          reads=[PS[2]], writes=R("negc"))

    head_load(0)
    for u in range(8):
        head_norm_unit(0, u)
    head_negc(0)
    for hh in range(HPC):
        qb_, kb_, vb_, ab_ = qh[hh % 2], kh[hh % 2], vh[hh % 2], aoh[hh % 2]
        for qb in range(16):
            if hh + 1 < HPC:
                if qb == 5:
                    head_load(hh + 1)
                elif 6 <= qb <= 13:
                    head_norm_unit(hh + 1, qb - 6)
                elif qb == 14:
                    head_negc(hh + 1)
            qs = slice(qb * 512, (qb + 1) * 512)
            npair = 2 * (qb + 1)

            def emit_S(p, hh=hh, kb_=kb_, qb_=qb_, qs=qs, npair=npair):
                sP = pp[p % 2]
                sB = [PS[2 * (p % 2)], PS[2 * (p % 2) + 1]]
                pt = ptb[p % 2]
                for j in range(2):
                    k0 = (2 * p + j) * 128
                    E("t", lambda e, sP=sP, j=j, kb_=kb_, qb_=qb_, k0=k0, qs=qs:
                        e.matmul(sP[:, j * 512:(j + 1) * 512],
                                 kb_[:, k0:k0 + 128], qb_[:, qs],
                                 start=True, stop=True),
                      reads=R(kb_.name, qb_.name), writes=[sB[j]])
                E("s", lambda e, pt=pt, sP=sP, hh=hh: e.activation(
                    pt[:, :], sP[:, :], AF.Exp, bias=negc[:, hh:hh + 1]),
                  reads=sB + R("negc"), writes=R(pt.name))
                if p >= npair - 2:
                    moff = (p - (npair - 2)) * 1024
                    E("v", lambda e, pt=pt, moff=moff: e.tensor_mul(
                        pt[:, :], pt[:, :], masks[:, moff:moff + 1024]),
                      reads=R(pt.name, "masks"), writes=R(pt.name))

            acc_b = 4 if qb % 2 == 0 else 6
            den_b = 5 if qb % 2 == 0 else 7

            def emit_PV(p, vb_=vb_, npair=npair, acc_b=acc_b, den_b=den_b):
                pt = ptb[p % 2]
                first, last = (p == 0), (p == npair - 1)
                for j in range(2):
                    k0 = (2 * p + j) * 128
                    E("t", lambda e, vb_=vb_, pt=pt, k0=k0, j=j, acc_b=acc_b,
                             st=(first and j == 0), sp=(last and j == 1): e.matmul(
                        pb[acc_b][:, :], vb_[:, k0:k0 + 128],
                        pt[:, j * 512:(j + 1) * 512], start=st, stop=sp),
                      reads=R(vb_.name, pt.name), writes=[PS[acc_b]])
                for j in range(2):
                    E("t", lambda e, pt=pt, j=j, den_b=den_b,
                             st=(first and j == 0), sp=(last and j == 1): e.matmul(
                        pb[den_b][0:1, :], ones_b[:, :],
                        pt[:, j * 512:(j + 1) * 512], start=st, stop=sp),
                      reads=R(pt.name, "ones_b"), writes=[PS[den_b]])

            emit_S(0)
            for p in range(npair):
                if p + 1 < npair:
                    emit_S(p + 1)
                emit_PV(p)
            rrow = nrow if qb % 2 == 0 else nrow2
            E("v", lambda e, den_b=den_b, rrow=rrow: e.reciprocal(
                rrow[0:1, 0:512], pb[den_b][0:1, :]),
              reads=[PS[den_b]], writes=R(rrow.name))
            E("t", lambda e, den_b=den_b, rrow=rrow: e.matmul(
                pb[den_b][:, :], ones_r[:, :], rrow[0:1, 0:512],
                start=True, stop=True),
              reads=R(rrow.name, "ones_r"), writes=[PS[den_b]])
            bc = bcs[qb % 2]
            E("s", lambda e, bc=bc, den_b=den_b: e.copy(bc[:, :], pb[den_b][:, :]),
              reads=[PS[den_b]], writes=R(bc.name))
            E("v", lambda e, ab_=ab_, qs=qs, bc=bc, acc_b=acc_b: e.tensor_mul(
                ab_[:, qs], pb[acc_b][:, :], bc[:, :]),
              reads=[PS[acc_b]] + R(bc.name), writes=R(ab_.name))
        D("g", a_pre.rearrange("(j f) t -> f j t", j=NC)[hh * 128:(hh + 1) * 128, :, :],
          ab_.ap().rearrange("p (j t) -> p j t", j=NC),
          reads=R(ab_.name), writes=[("dr", "a_pre")])

    E("g", lambda e: e.dma_start(out=a_st, in_=a_pre),
      reads=[("dr", "a_pre")], writes=[("dr", "a_st")], kind="c")
    D("g", zB[:, 0:HS], u1_r[0], reads=[("dr", "u1_r")], writes=R("zB"))
    D("g", zB[:, HS:2 * HS], u1_r[1], reads=[("dr", "u1_r")], writes=R("zB"))
    for i in (0, 1):
        D("g", v1g[i][:, :], v1_r[i], reads=[("dr", "v1_r")], writes=R(v1g[i].name))
        D("g", v1u[i][:, :], v1_r[NFC + i], reads=[("dr", "v1_r")], writes=R(v1u[i].name))
    for gi in range(G_FFN):
        D("g", u2g[gi][:, :], u2_r[gi], reads=[("dr", "u2_r")], writes=R(u2g[gi].name))
    E("g", lambda e: e.collective_compute(
        "AllGather", ALU.bypass, replica_groups=[list(range(NC))],
        ins=[a_st], outs=[a_g]),
      reads=[("dr", "a_st")], writes=[("dr", "a_g")], kind="cc")
    E("g", lambda e: e.collective_compute(
        "AllReduce", ALU.add, replica_groups=[list(range(NC))],
        ins=[bar2i], outs=[bar2o]),
      reads=[("dr", "a_g")], writes=[("dr", "a_g"), ("dr", "bar2")], kind="cc")

    E("v", lambda e: e.memset(zS[:, :], 0.0), writes=R("zS"))
    for half2 in range(2):
        D("y", xno.ap()[:, half2 * 16 * 1024:(half2 + 1) * 16 * 1024],
          xn_st[half2 * 2048:(half2 + 1) * 2048, :].rearrange("(hc p) t -> p hc t", p=128),
          reads=[("dr", "xn_st")], writes=R("xno"))
    for rc in range(RC):
        ub = zB.ap()[:, (rc % 2) * HS:(rc % 2 + 1) * HS]
        if rc > 1:
            D("g", ub, u1_r[rc], reads=[("dr", "u1_r")], writes=R("zB"))
        for half in range(2):
            bank = 2 * (rc % 2) + half
            for hc in range(NHC):
                E("t", lambda e, ub=ub, hc=hc, half=half, bank=bank,
                         st=(hc == 0), sp=(hc == NHC - 1): e.matmul(
                    pb[bank][:, :], ub[:, hc * 128:(hc + 1) * 128],
                    xno[:, hc * 1024 + half * 512:hc * 1024 + (half + 1) * 512],
                    start=st, stop=sp),
                  reads=R("zB", "xno"), writes=[PS[bank]])
            E("s", lambda e, rc=rc, half=half, bank=bank: e.copy(
                p1[:, rc * 1024 + half * 512:rc * 1024 + (half + 1) * 512],
                pb[bank][:, :]),
              reads=[PS[bank]], writes=R("p1"))
    fgroups = []
    i0 = 0
    while i0 < NFC:
        fgroups.append(list(range(i0, min(i0 + G_FFN, NFC))))
        i0 += G_FFN
    for grp in fgroups:
        for gi, i in enumerate(grp):
            vg, vu = v1g[i % 2], v1u[i % 2]
            ag = actg[gi]
            if i > 1:
                D("g", vg[:, :], v1_r[i], reads=[("dr", "v1_r")], writes=R(vg.name))
                D("g", vu[:, :], v1_r[NFC + i], reads=[("dr", "v1_r")], writes=R(vu.name))
            if i >= G_FFN:
                D("g", u2g[gi][:, :], u2_r[i], reads=[("dr", "u2_r")],
                  writes=R(u2g[gi].name))
            for half in range(2):
                gb, ub_ = half, 2 + half
                for rc in range(RC):
                    E("t", lambda e, vg=vg, rc=rc, half=half, gb=gb,
                             st=(rc == 0), sp=(rc == RC - 1): e.matmul(
                        pb[gb][:, :], vg[:, rc * 128:(rc + 1) * 128],
                        p1[:, rc * 1024 + half * 512:rc * 1024 + (half + 1) * 512],
                        start=st, stop=sp),
                      reads=R(vg.name, "p1"), writes=[PS[gb]])
                    E("t", lambda e, vu=vu, rc=rc, half=half, ub_=ub_,
                             st=(rc == 0), sp=(rc == RC - 1): e.matmul(
                        pb[ub_][:, :], vu[:, rc * 128:(rc + 1) * 128],
                        p1[:, rc * 1024 + half * 512:rc * 1024 + (half + 1) * 512],
                        start=st, stop=sp),
                      reads=R(vu.name, "p1"), writes=[PS[ub_]])
                E("s", lambda e, half=half, gb=gb, i=i: e.activation(
                    silub[:, half * 512:(half + 1) * 512], pb[gb][:, :], AF.Silu,
                    bias=b1s[:, i:i + 1]),
                  reads=[PS[gb]] + R("b1s"), writes=R("silub"))
                E("v", lambda e, ag=ag, half=half, ub_=ub_, i=i: e.scalar_tensor_tensor(
                    ag[:, half * 512:(half + 1) * 512], pb[ub_][:, :],
                    b1s[:, NFC + i:NFC + i + 1],
                    silub[:, half * 512:(half + 1) * 512], ALU.add, ALU.mult),
                  reads=[PS[ub_]] + R("b1s", "silub"), writes=R(ag.name))
        for rc in range(RC):
            for half in range(2):
                bank = 4 + half
                for gi, i in enumerate(grp):
                    ag = actg[gi]
                    ub2 = u2g[gi]
                    E("t", lambda e, ub2=ub2, ag=ag, rc=rc, half=half, bank=bank,
                             st=(gi == 0), sp=(gi == len(grp) - 1): e.matmul(
                        pb[bank][:, :], ub2[:, rc * 128:(rc + 1) * 128],
                        ag[:, half * 512:(half + 1) * 512], start=st, stop=sp),
                      reads=R(ub2.name, ag.name), writes=[PS[bank]])
                E("v", lambda e, rc=rc, half=half, bank=bank: e.tensor_add(
                    zS[:, rc * 1024 + half * 512:rc * 1024 + (half + 1) * 512],
                    zS[:, rc * 1024 + half * 512:rc * 1024 + (half + 1) * 512],
                    pb[bank][:, :]),
                  reads=[PS[bank]] + R("zS"), writes=R("zS"))
    E("v", lambda e: e.tensor_copy(zB[:, :], zS[:, :]), reads=R("zS"), writes=R("zB"))

    asl_pg = pages["aslab"]
    for hi in range(NHC):
        E("g", lambda e, hi=hi: e.indirect_dma_start(
            aslab[:, hi * 1024:(hi + 1) * 1024], None, a_g,
            IndirectOffsetOnAxis(idxs[:, hi:hi + 1], 0)),
          reads=[("dr", "a_g")] + R("idxs"),
          writes=list(asl_pg[hi * 2:(hi + 1) * 2]), kind="d")
    for ho in range(NHC):
        wb = wob[ho % 2]
        v2 = v2b[ho % 2]
        xc = xfc[ho % 2]
        oS = outS[ho % 2]
        D("g", wb[:, :], wo_r[ho], reads=[("dr", "wo_r")], writes=R(wb.name))
        D("g", v2[:, :], v2_r[ho], reads=[("dr", "v2_r")], writes=R(v2.name))
        D("y", xc[:, :], xf[ho * 128:(ho + 1) * 128, :],
          reads=[("dr", "xf")], writes=R(xc.name))
        for half in range(2):
            bank = 2 * (ho % 2) + half
            for hi in range(NHC):
                E("t", lambda e, wb=wb, hi=hi, half=half, bank=bank,
                         st=(hi == 0): e.matmul(
                    pb[bank][:, :], wb[:, hi * 128:(hi + 1) * 128],
                    aslab[:, hi * 1024 + half * 512:hi * 1024 + (half + 1) * 512],
                    start=st, stop=False),
                  reads=R(wb.name) + list(asl_pg[hi * 2:(hi + 1) * 2]),
                  writes=[PS[bank]])
            for rc in range(RC):
                E("t", lambda e, v2=v2, rc=rc, half=half, bank=bank,
                         sp=(rc == RC - 1): e.matmul(
                    pb[bank][:, :], v2[:, rc * 128:(rc + 1) * 128],
                    zB[:, rc * 1024 + half * 512:rc * 1024 + (half + 1) * 512],
                    start=False, stop=sp),
                  reads=R(v2.name, "zB"), writes=[PS[bank]])
            E("v", lambda e, oS=oS, xc=xc, half=half, bank=bank, ho=ho: e.scalar_tensor_tensor(
                oS[:, half * 512:(half + 1) * 512], pb[bank][:, :],
                b2s[:, ho:ho + 1], xc[:, half * 512:(half + 1) * 512],
                ALU.add, ALU.add),
              reads=[PS[bank]] + R(xc.name, "b2s"), writes=R(oS.name))
        D("y", out_t[ho * 128:(ho + 1) * 128, :], oS[:, :],
          reads=R(oS.name), writes=[("dr", "out")])
    if dbg:
        for nm, src_ap in (("xn_st", xn_st), ("xn_g", xn_g), ("q_st", q_st),
                           ("k_st", k_st), ("v_st", v_st), ("a_st", a_st),
                           ("a_g", a_g)):
            D("g", dbg_t[nm], src_ap, reads=[("dr", nm.replace("q_st", "qst").replace("k_st", "kst").replace("v_st", "vst"))],
              writes=[("dr", "d" + nm)])
    dbg_reads = [("dr", "d" + nm) for nm in dbg_t] if dbg else []
    E("y", None, reads=[("dr", "out")] + dbg_reads)

    b.finalize()
    return nc




_cache = {}


def _host_prep():
    if "c" in _cache:
        return _cache["c"]
    inv_freq = 1.0 / 10000.0 ** (np.arange(0, HD, 2, dtype=np.float32) / HD)
    pos = np.arange(SL, dtype=np.float32)
    emb = np.concatenate([pos[:, None] * inv_freq[None, :]] * 2, -1)
    cos_t = np.ascontiguousarray(np.cos(emb).astype(np.float32).T)
    sin_t = np.ascontiguousarray(np.sin(emb).astype(np.float32).T)
    m = np.zeros((128, 4 * 512), np.float32)
    for kt in range(4):
        m[:, kt * 512:(kt + 1) * 512] = (
            kt * 128 + np.arange(128)[:, None] <= np.arange(512)[None, :])
    mask_t = m.astype(BF)
    idxs = []
    for c in range(NC):
        idx = np.zeros((128, NHC), np.int32)
        for hi in range(NHC):
            r, hh = hi // HPC, hi % HPC
            idx[:, hi] = r * HS + c * 512 + hh * 128 + np.arange(128)
        idxs.append(idx)
    _cache["c"] = (cos_t, sin_t, mask_t, idxs)
    return _cache["c"]


def kernel(L0_attn_hidden_states, L0_attn_wq, L0_attn_wk, L0_attn_wv, L0_attn_wo,
           L0_ffn_U1, L0_ffn_V1, L0_ffn_U2, L0_ffn_V2, L0_ffn_b1, L0_ffn_b2):
    x = np.asarray(L0_attn_hidden_states, np.float32).reshape(SL, HS)
    wq = np.asarray(L0_attn_wq, np.float32)
    wk = np.asarray(L0_attn_wk, np.float32)
    wv = np.asarray(L0_attn_wv, np.float32)
    wo = np.asarray(L0_attn_wo, np.float32)
    U1 = np.asarray(L0_ffn_U1, np.float32)
    V1 = np.asarray(L0_ffn_V1, np.float32)
    U2 = np.asarray(L0_ffn_U2, np.float32)
    V2 = np.asarray(L0_ffn_V2, np.float32)
    b1 = np.asarray(L0_ffn_b1, np.float32)
    b2 = np.asarray(L0_ffn_b2, np.float32)

    cos_t, sin_t, mask_t, idxs = _host_prep()
    wo_r = np.ascontiguousarray(
        wo.reshape(NHC, 128, NHC, 128).transpose(2, 1, 0, 3).reshape(NHC, 128, HS))
    u1_r = np.ascontiguousarray(
        U1.reshape(NHC, 128, RC, 128).transpose(2, 1, 0, 3).reshape(RC, 128, HS))
    v1_r = np.ascontiguousarray(
        V1.reshape(RC, 128, 2 * NFC, 128).transpose(2, 1, 0, 3).reshape(2 * NFC, 128, RANK))
    u2_r = np.ascontiguousarray(U2.reshape(NFC, 128, RANK))
    v2_r = np.ascontiguousarray(
        V2.reshape(RC, 128, NHC, 128).transpose(2, 1, 0, 3).reshape(NHC, 128, RANK))
    b1_t = np.ascontiguousarray(b1.reshape(2 * NFC, 128).T)
    b2_t = np.ascontiguousarray(b2.reshape(NHC, 128).T)

    nc = build_nc()
    in_maps = []
    for c in range(NC):
        hsl = slice(c * HPC * HD, (c + 1) * HPC * HD)
        in_maps.append(dict(
            xf=np.ascontiguousarray(x[c * SLAB:(c + 1) * SLAB].T),
            wq=np.ascontiguousarray(wq[:, hsl]),
            wk=np.ascontiguousarray(wk[:, hsl]),
            wv=np.ascontiguousarray(wv[:, hsl]),
            wo_r=wo_r, u1_r=u1_r, v1_r=v1_r, u2_r=u2_r, v2_r=v2_r,
            b1_t=b1_t, b2_t=b2_t, cos_t=cos_t, sin_t=sin_t,
            mask_t=mask_t, idx_t=idxs[c]))
    res = run_bass_kernel_spmd(nc, in_maps, core_ids=list(range(NC)))
    out = np.concatenate([res.results[c]["out"].T for c in range(NC)], 0)
    return out.reshape(1, SL, HS)
